# version 06
# baseline (speedup 1.0000x reference)
import math
import numpy as np
import ml_dtypes
import concourse.bass as bass
import concourse.mybir as mybir
from concourse.bass_utils import run_bass_kernel_spmd

F32 = mybir.dt.float32
BF16 = mybir.dt.bfloat16
U32 = mybir.dt.uint32
I32 = mybir.dt.int32
U8 = mybir.dt.uint8
AF = mybir.ActivationFunctionType
ALU = mybir.AluOpType
AXX = mybir.AxisListType.X

DM, NDT, S, B = 4096, 32, 2048, 4
NH, NKV, HD = 16, 4, 128
NCHUNK, CW, T = 4, 512, 1024
NPAIR, UPC = 86, 11
GROW = 32 * 128 * 256
DROW = 128 * 4096
RSQ = 1.0 / math.sqrt(128.0)
NEG = -1.0e30
KB = 1024
AVPIECE = 128 * CW
AVREGION = NH * NCHUNK * AVPIECE


def _slopes():
    start = 2 ** (-(2 ** -(math.log2(32) - 3)))
    return np.array([start * start ** i for i in range(32)], dtype=np.float32)


class Ch:
    def __init__(self, nc, name):
        self.h = nc.alloc_semaphore(name)
        self.n = 0


class Plan:
    def __init__(self):
        self.ev = []

    def e(self, eng, fn=None, waits=(), inc=None, chain=None):
        self.ev.append((eng, fn, list(waits), inc, chain))

    def run(self, name, eng):
        for (engine, fn, waits, inc, chain) in self.ev:
            if engine != name:
                continue
            if chain is not None and chain[1] > 0:
                eng.wait_ge(chain[0].h, chain[1])
            for (ch, val) in waits:
                if val > 0:
                    eng.wait_ge(ch.h, val)
            if fn is not None:
                inst = fn(eng)
                if inc is not None:
                    ch, amt = inc
                    inst.then_inc(ch.h, amt)

    def emit(self, nc):
        with nc.Block() as blk:
            @blk.sync
            def _(e):
                self.run('y', e)

            @blk.gpsimd
            def _(e):
                self.run('g', e)

            @blk.tensor
            def _(e):
                self.run('t', e)

            @blk.vector
            def _(e):
                self.run('v', e)

            @blk.scalar
            def _(e):
                self.run('s', e)


def build_program(phases='ABCD', detect_races=True):
    nc = bass.Bass("TRN2", target_bir_lowering=False, detect_race_conditions=detect_races)
    nc.cache_partition_id()

    def din(name, shape, dt):
        return nc.dram_tensor(name, shape, dt, kind="ExternalInput")

    xT = din("xT_b", [DM, S], F32)
    xown = din("xown", [DM, T], F32)
    wqT = din("wqT_s", [DM, NH * HD], F32)
    wkT = din("wkT_s", [DM, NKV * HD], F32)
    wvT = din("wvT_s", [DM, NKV * HD], F32)
    woT = din("woT", [DM, DM], F32)
    qnorm_i = din("qnorm", [128, 1], F32)
    knorm_i = din("knorm", [128, 1], F32)
    srow_in = din("srow_in", [NH * NCHUNK, CW], F32)
    bcol_in = din("bcol_in", [128, NH * 16], F32)
    trimask_i = din("trimask", [128, 128], BF16)
    ident_i = din("ident128", [128, 128], BF16)
    onesrow_i = din("onesrow", [1, 128], BF16)
    onescol_i = din("onescol", [128, 1], BF16)
    maskcol_i = din("maskcol", [128, 1], U32)
    pw2col_i = din("pw2col", [128, 1], F32)
    qw_gu = din("qw_gu_s", [4096, 256 * UPC], U32)
    sc_gu = din("sc_gu_s", [32, 256 * UPC], F32)
    qz_gu = din("qz_gu_s", [32, 32 * UPC], U32)
    qw_d = din("qw_d_s", [128 * UPC, 4096], U32)
    sc_d = din("sc_d_s", [UPC, 4096], F32)
    qz_d = din("qz_d_s", [UPC, 512], U32)

    out_xt = nc.dram_tensor("out_xt", [DM, T], F32, kind="ExternalOutput")
    if 'DBG' in phases:
        dbg_out = nc.dram_tensor("dbg_out", [128, 16 * CW], F32, kind="ExternalOutput")

    shard_gu = nc.dram_tensor("shard_gu", [UPC, GROW], BF16)
    ggu = nc.dram_tensor("ggu", [8 * UPC, GROW], BF16)
    shard_d = nc.dram_tensor("shard_d", [UPC, DROW], BF16)
    gd = nc.dram_tensor("gd", [8 * UPC, DROW], BF16)
    avx = nc.dram_tensor("avx", [2, AVREGION], BF16, addr_space="Shared")
    avmine = nc.dram_tensor("avmine", [1, AVREGION], BF16)
    avloc = nc.dram_tensor("avloc", [1, 32 * 2 * AVPIECE], BF16)
    x2d = nc.dram_tensor("x2d", [DM, T], BF16)
    zscr = nc.dram_tensor("zscr", [1, 8192], F32)
    rscr = nc.dram_tensor("rscr", [1, 2048], F32)
    bar_i = nc.dram_tensor("bar_i", [1, 16], F32)
    bar_o = nc.dram_tensor("bar_o", [1, 16], F32)

    qnorm = nc.alloc_sbuf_tensor("qnorm_sb", [128, 1], F32)
    knorm = nc.alloc_sbuf_tensor("knorm_sb", [128, 1], F32)
    bcol = nc.alloc_sbuf_tensor("bcol_sb", [128, NH * 16], F32)
    trimask = nc.alloc_sbuf_tensor("trimask_sb", [128, 128], BF16)
    ident = nc.alloc_sbuf_tensor("ident_sb", [128, 128], BF16)
    junk = nc.alloc_sbuf_tensor("junk_sb", [1, 16], F32)
    onesrow = nc.alloc_sbuf_tensor("onesrow_sb", [1, 128], BF16)
    onescol = nc.alloc_sbuf_tensor("onescol_sb", [128, 1], BF16)
    maskcol = nc.alloc_sbuf_tensor("maskcol_sb", [128, 1], U32)
    pw2col = nc.alloc_sbuf_tensor("pw2col_sb", [128, 1], F32)
    srow_t = nc.alloc_sbuf_tensor("srow_t", [1, CW], F32)
    rowf = nc.alloc_sbuf_tensor("rowf_sb", [1, 2048], F32)
    rowb = nc.alloc_sbuf_tensor("rowb_sb", [1, 1536], BF16)
    maxk = nc.alloc_sbuf_tensor("maxk_sb", [1, 8], F32)
    bar_sb = nc.alloc_sbuf_tensor("bar_sb", [1, 16], F32)
    eps6 = nc.alloc_sbuf_tensor("eps6_sb", [1, 1], F32)
    eps5 = nc.alloc_sbuf_tensor("eps5_sb", [1, 1], F32)
    residual = nc.alloc_sbuf_tensor("residual", [128, 32 * T], BF16)

    ARENA = 118 * KB
    arena = nc.alloc_sbuf_tensor("arena", [128, ARENA], U8)

    def AR(off, nbytes, dt, parts=128):
        assert off + nbytes <= ARENA, (off, nbytes)
        return arena[0:parts, off:off + nbytes].bitcast(dt)

    def RS(off, nbytes, dt, parts=128):
        assert off + nbytes <= 64 * KB, (off, nbytes)
        ap = residual[0:parts, off // 2:(off + nbytes) // 2]
        return ap if dt == BF16 else ap.bitcast(dt)

    pbk = [nc.alloc_psum_tensor(f"pb{i}", [128, 512], F32) for i in range(8)]

    dg = Ch(nc, "dg")
    dy = Ch(nc, "dy")
    cag1 = Ch(nc, "cag1")
    cag2 = Ch(nc, "cag2")
    cbar = Ch(nc, "cbar")
    pe = Ch(nc, "pe")
    dv = Ch(nc, "dv")
    ac = Ch(nc, "ac")

    P = Plan()
    _regs = {}

    def goff(e, mul, const):
        if 'g' not in _regs:
            rpar = e.to_reg(e.partition_id() % 2)
            roff = e.alloc_register('roff')
            _regs['g'] = (rpar, roff)
        rpar, roff = _regs['g']
        e.reg_mul(roff, rpar, mul)
        e.reg_add(roff, roff, const)
        return roff

    def dma(q, out, in_, waits=()):
        ch = dg if q == 'g' else dy
        P.e(q, lambda e, o=out, i=in_: e.dma_start(out=o, in_=i), waits, (ch, 16),
            chain=(ch, ch.n))
        ch.n += 16
        return ch.n

    def mm(out, lhsT, rhs, start, stop, waits=(), tok=False):
        do_inc = stop or tok
        def f(e, o=out, l=lhsT, r=rhs, st=start, sp=stop):
            return e.matmul(o, l, r, start=st, stop=sp)
        if do_inc:
            pe.n += 1
        P.e('t', f, waits, (pe, 1) if do_inc else None)
        return pe.n

    def vop(fn, waits=()):
        dv.n += 1
        P.e('v', fn, waits, (dv, 1), chain=(dv, dv.n - 1))
        return dv.n

    def sop(fn, waits=()):
        ac.n += 1
        P.e('s', fn, waits, (ac, 1), chain=(ac, ac.n - 1))
        return ac.n

    def settle():
        return vop(lambda e: e.memset(junk[0:1, :], 0.0))

    if 'NOCONSTS' not in phases:
        for (dst, src) in [
            (qnorm[:, :], qnorm_i.ap()), (knorm[:, :], knorm_i.ap()),
            (bcol[:, :], bcol_in.ap()), (trimask[:, :], trimask_i.ap()),
            (onesrow[:, :], onesrow_i.ap()), (onescol[:, :], onescol_i.ap()),
            (maskcol[:, :], maskcol_i.ap()), (pw2col[:, :], pw2col_i.ap()),
        (ident[:, :], ident_i.ap()),
        ]:
            dma('y', dst, src)
    t_consts = dy.n
    if 'NOEPS' not in phases:
        vop(lambda e: e.memset(eps6[:, :], 1e-6))
        vop(lambda e: e.memset(eps5[:, :], 1e-5))

    if 'NORES' not in phases:
        dma('g', residual[:, :], bass.AP(xown, 0, [[T, 128], [128 * T, NDT], [1, T]]))

    wst = AR(0, 16 * KB, U32)
    nib = AR(16 * KB, 16 * KB, U32)
    sce = AR(32 * KB, 16 * KB, F32)
    zse = AR(48 * KB, 16 * KB, F32)
    wb1 = AR(64 * KB, 16 * KB, F32)
    wb2 = AR(80 * KB, 8 * KB, BF16)
    zqu = AR(88 * KB, 1 * KB, U32, parts=32)
    zqw = AR(89 * KB, 128, U32, parts=32)
    scu = AR(90 * KB, 1 * KB, F32, parts=32)[:, 0:256]
    tmpu = AR(91 * KB, 1 * KB, F32, parts=32)[:, 0:256]
    zsu = AR(92 * KB, 1 * KB, F32, parts=32)[:, 0:256]
    zrow_u = AR(93 * KB, 2 * KB, U32, parts=1)
    zrow_q = AR(95 * KB, 16 * KB, U32, parts=1)
    zrow_s = AR(64 * KB, 16 * KB, F32, parts=1)
    zrow_f = AR(48 * KB, 16 * KB, F32, parts=1)
    zrow_b = AR(16 * KB, 16 * KB, F32, parts=1)

    g_w_dv = 0
    g_small = 0
    g_zstore = 0
    prev_store = 0
    for u in range([] if 'NODEQ' in phases else range(UPC)) if False else (range(0) if 'NODEQ' in phases else range(UPC)):
        tz = dma('y', zqw, bass.AP(qz_gu, u * 32, [[32 * UPC, 32], [1, 32]]),
                 waits=((dv, g_small),))
        tsc = dma('g', scu, bass.AP(sc_gu, u * 256, [[256 * UPC, 32], [1, 256]]),
                  waits=((dv, g_small),))
        zq3u = zqu.rearrange("g (j s) -> g j s", s=8)
        for sft in range(8):
            vop(lambda e, s_=sft: e.tensor_scalar(
                out=zq3u[:, :, s_:s_ + 1], in0=zqw, scalar1=4 * s_, scalar2=15,
                op0=ALU.logical_shift_right, op1=ALU.bitwise_and),
                waits=((dy, tz),) if sft == 0 else ())
        vop(lambda e: e.scalar_tensor_tensor(
            out=tmpu, in0=zqu, scalar=1.0, in1=scu, op0=ALU.add, op1=ALU.mult),
            waits=((dg, tsc),))
        t_zs = vop(lambda e: e.tensor_scalar(
            out=zsu, in0=tmpu, scalar1=-1.0, scalar2=None, op0=ALU.mult),
            waits=((dy, g_zstore),))
        t_zss = settle()
        t_zw = dma('y', bass.AP(zscr, 0, [[256, 32], [1, 256]]), zsu,
                   waits=((dv, t_zss), (dg, dg.n)))

        for half in range(2):
            fc = u * 256 + half * 128
            tw = dma('y', wst,
                     bass.AP(qw_gu, fc, [[256 * UPC, 128],
                                         [128 * 256 * UPC, 32], [1, 128]]),
                     waits=((dv, g_w_dv),))
            ts = dma('g', sce, bass.AP(sc_gu, fc, [[0, 128], [256 * UPC, 32], [1, 128]]),
                     waits=((dv, g_w_dv),))
            tz2 = dma('g', zse, bass.AP(zscr, half * 128, [[0, 128], [256, 32], [1, 128]]),
                      waits=((dy, t_zw), (dv, g_w_dv)))
            vop(lambda e: e.tensor_scalar(
                out=nib, in0=wst, scalar1=maskcol[:, :], scalar2=None,
                op0=ALU.bitwise_and),
                waits=((dy, tw), (dy, t_consts)))
            vop(lambda e: e.scalar_tensor_tensor(
                out=wb1, in0=nib, scalar=pw2col[:, :], in1=sce,
                op0=ALU.mult, op1=ALU.mult), waits=((dg, ts),))
            wv = vop(lambda e: e.tensor_tensor(out=wb2, in0=wb1, in1=zse, op=ALU.add),
                     waits=((dg, tz2), (dy, prev_store)))
            g_w_dv = wv
            wvs = settle()
            prev_store = dma('y', bass.AP(shard_gu, u * GROW + half * 128,
                                          [[256, 128], [128 * 256, 32], [1, 128]]),
                             wb2, waits=((dv, wvs),))
        g_small = t_zs
        g_zstore = t_zw

        tzr = dma('y', zrow_u, bass.AP(qz_d, u * 512, [[512, 1], [1, 512]]),
                  waits=((dv, g_small),))
        tss = dma('y', zrow_s, bass.AP(sc_d, u * 4096, [[4096, 1], [1, 4096]]))
        zr3 = zrow_q.rearrange("g (j s) -> g j s", s=8)
        for sft in range(8):
            vop(lambda e, s_=sft: e.tensor_scalar(
                out=zr3[:, :, s_:s_ + 1], in0=zrow_u, scalar1=4 * s_,
                scalar2=15, op0=ALU.logical_shift_right, op1=ALU.bitwise_and),
                waits=((dy, tzr),) if sft == 0 else ())
        vop(lambda e: e.scalar_tensor_tensor(
            out=zrow_f, in0=zrow_q, scalar=1.0, in1=zrow_s,
            op0=ALU.add, op1=ALU.mult), waits=((dy, tss), (dy, prev_store)))
        t_zd = vop(lambda e: e.tensor_scalar(
            out=zrow_b, in0=zrow_f, scalar1=-1.0, scalar2=None, op0=ALU.mult))
        t_zds = settle()
        t_zw2 = dma('y', bass.AP(zscr, 0, [[4096, 1], [1, 4096]]), zrow_b,
                    waits=((dv, t_zds), (dg, dg.n)))
        g_small = t_zd
        g_zstore = t_zw2
        tw = dma('y', wst, bass.AP(qw_d, u * 128 * 4096, [[4096, 128], [1, 4096]]),
                 waits=((dv, g_w_dv),))
        ts = dma('g', sce, bass.AP(sc_d, u * 4096, [[0, 128], [1, 4096]]),
                 waits=((dv, g_w_dv),))
        tz2 = dma('g', zse, bass.AP(zscr, 0, [[0, 128], [1, 4096]]),
                  waits=((dy, t_zw2), (dv, g_w_dv)))
        vop(lambda e: e.tensor_scalar(
            out=nib, in0=wst, scalar1=maskcol[:, :], scalar2=None,
            op0=ALU.bitwise_and), waits=((dy, tw), (dy, t_zw2)))
        vop(lambda e: e.scalar_tensor_tensor(
            out=wb1, in0=nib, scalar=pw2col[:, :], in1=sce,
            op0=ALU.mult, op1=ALU.mult), waits=((dg, ts),))
        wv = vop(lambda e: e.tensor_tensor(out=wb2, in0=wb1, in1=zse, op=ALU.add),
                 waits=((dg, tz2), (dy, prev_store)))
        g_w_dv = wv
        wvs = settle()
        prev_store = dma('y', bass.AP(shard_d, u * DROW, [[4096, 128], [1, 4096]]),
                         wb2, waits=((dv, wvs),))
    t_shards = dy.n

    P.e('g', lambda e: e.collective_compute(
        "AllGather", ALU.bypass, replica_groups=[list(range(8))],
        ins=[shard_gu.ap().opt()], outs=[ggu.ap().opt()]),
        ((dy, t_shards),), (cag1, 1))
    cag1.n += 1
    P.e('g', lambda e: e.collective_compute(
        "AllGather", ALU.bypass, replica_groups=[list(range(8))],
        ins=[shard_d.ap().opt()], outs=[gd.ap().opt()]),
        (), (cag2, 1))
    cag2.n += 1

    P.e('y', None, ((dy, dy.n),))
    P.e('g', None, ((dg, dg.n), (dv, dv.n)))
    P.emit(nc)
    P = Plan()

    if phases.startswith('A') and 'B' not in phases:
        P.e('g', lambda e: e.dma_start(out=out_xt.ap(), in_=bass.AP(ggu, 0, [[T, DM], [1, T]])), ((cag1, 1), (cag2, 1)), (dg, 16))
        dg.n += 16
        P.e('y', None, ((dy, dy.n),))
        P.emit(nc)
        return nc
    Khat = AR(0, 16 * KB, BF16)
    Vb = AR(16 * KB, 16 * KB, BF16)
    xc = AR(32 * KB, 32 * KB, BF16)
    wstg = [AR(64 * KB + i * 8 * KB, 8 * KB, BF16) for i in range(2)]
    probs = AR(80 * KB, 16 * KB, BF16)
    s1b = [AR(96 * KB + i * 2 * KB, 2 * KB, F32) for i in range(2)]
    shex = AR(100 * KB, 2 * KB, F32)
    invex = AR(102 * KB, 2 * KB, F32)
    kraw = AR(104 * KB, 1 * KB, BF16)
    k2b = AR(105 * KB, 1 * KB, BF16)
    qhat = AR(106 * KB, 1 * KB, BF16)
    avb = AR(107 * KB, 1 * KB, BF16)
    wvstg = [AR(108 * KB + i * KB, 1 * KB, BF16) for i in range(2)]
    recb = AR(110 * KB, 1 * KB, BF16)

    PJ, PX = pbk[0], pbk[1]
    PV = [pbk[0], pbk[1], pbk[2], pbk[3]]
    PS = [pbk[4], pbk[5]]
    PA = pbk[6]
    PR = pbk[7]

    v_tok = {}
    g_wstg = [0, 0]
    g_shex = 0
    g_invex = 0
    g_qhat = 0
    g_probs = 0
    g_avb = 0
    g_PR = (dv, 0)
    g_PA = 0
    g_PJ = (ac, 0)
    g_PX = 0
    g_rows_pe = 0
    g_srow = 0
    v_stage = [0, 0]

    def rms_rows(tsum):
        t_ln = sop(lambda e: e.activation(out=rowf[0:1, 0:CW], in_=PR[0:1, :],
                                          func=AF.Ln, bias=eps6[0:1, 0:1], scale=1.0 / 128.0),
                   waits=((pe, tsum), (pe, g_rows_pe)))
        t_ex = sop(lambda e: e.activation(out=rowb[0:1, 0:CW], in_=rowf[0:1, 0:CW],
                                          func=AF.Exp, scale=-0.5))
        return t_ln, t_ex

    import re as _re
    _mc = _re.search(r'C(\d+)', phases)
    _mh = _re.search(r'H(\d+)', phases)
    _nc_lim = int(_mc.group(1)) if _mc else NCHUNK
    _nh_lim = int(_mh.group(1)) if _mh else NH
    for c in range(_nc_lim):
        KT = 4 * (c + 1)
        t_xc = dma('g', xc, bass.AP(xT, c * CW, [[S, 128], [128 * S, NDT], [1, CW]]),
                   waits=((pe, pe.n),))
        for kv in range(NKV):
            tws = dma('g', wstg[kv % 2],
                      bass.AP(wkT, kv * 128, [[NKV * HD, 128],
                                              [128 * NKV * HD, 32], [1, 128]]),
                      waits=((pe, g_wstg[kv % 2]),))
            for d in range(NDT):
                w = [(dg, tws), (dg, t_xc), g_PJ] if d == 0 else ()
                tk = mm(PJ[:, :], wstg[kv % 2][:, d * 128:(d + 1) * 128],
                        xc[:, d * CW:(d + 1) * CW], start=(d == 0),
                        stop=(d == NDT - 1), waits=w)
            g_wstg[kv % 2] = tk
            t_kc = sop(lambda e: e.copy(kraw, PJ[:, :]),
                       waits=((pe, tk), (dv, g_invex)))
            t_k2 = sop(lambda e: e.activation(out=k2b, in_=kraw, func=AF.Square))
            tsum = mm(PR[0:1, :], onescol[:, :], k2b, start=True, stop=True,
                      waits=[(ac, t_k2), g_PR])
            t_ln, t_ex = rms_rows(tsum)
            g_PR = (ac, t_ln)
            tex = mm(PX[:, :], onesrow[:, :], rowb[0:1, 0:CW], start=True, stop=True,
                     waits=((ac, t_ex), (ac, g_PX)))
            g_rows_pe = tex
            t_ic = sop(lambda e: e.copy(invex, PX[:, :]),
                       waits=((pe, tex), (dv, g_invex)))
            g_PX = t_ic
            t_kh = vop(lambda e, kvv=kv, cc=c: e.scalar_tensor_tensor(
                out=Khat[:, kvv * 2048 + cc * CW: kvv * 2048 + (cc + 1) * CW],
                in0=PJ[:, :], scalar=knorm[:, :], in1=invex, op0=ALU.mult, op1=ALU.mult),
                waits=((ac, t_ic),))
            g_invex = t_kh
            g_PJ = (dv, t_kh)
            t_khs = settle()
            t_k2n = sop(lambda e, kvv=kv, cc=c: e.activation(
                out=k2b, in_=Khat[:, kvv * 2048 + cc * CW: kvv * 2048 + (cc + 1) * CW],
                func=AF.Square), waits=((dv, t_khs),))
            tsum = mm(PR[0:1, :], onescol[:, :], k2b, start=True, stop=True,
                      waits=[(ac, t_k2n), g_PR])
            t_mx = vop(lambda e: e.tensor_reduce(rowf[0:1, CW:CW + 1], PR[0:1, :],
                                                 AXX, ALU.max), waits=((pe, tsum),))
            g_PR = (dv, t_mx)
            if c == 0:
                vop(lambda e, kvv=kv: e.tensor_copy(maxk[0:1, kvv:kvv + 1],
                                                    rowf[0:1, CW:CW + 1]))
            else:
                vop(lambda e, kvv=kv: e.tensor_tensor(
                    out=maxk[0:1, kvv:kvv + 1], in0=maxk[0:1, kvv:kvv + 1],
                    in1=rowf[0:1, CW:CW + 1], op=ALU.max))

        for d in range(NDT):
            tvw = dma('g', wvstg[d % 2],
                      bass.AP(wvT, d * 128 * NKV * HD, [[NKV * HD, 128], [1, NKV * HD]]),
                      waits=((pe, pe.n if d < 2 else v_stage[d % 2]),))
            for sub in range(4):
                w = []
                if d == 0 and sub == 0:
                    w = [g_PJ, (ac, g_PX), (dg, tvw)]
                elif sub == 0:
                    w = [(dg, tvw)]
                tkk = mm(PV[sub][:, :], xc[:, d * CW + sub * 128: d * CW + sub * 128 + 128],
                         wvstg[d % 2], start=(d == 0), stop=(d == NDT - 1),
                         waits=w, tok=(sub == 3))
            v_stage[d % 2] = tkk
        for sub in range(4):
            tvc = sop(lambda e, ss=sub, cc=c: e.copy(
                Vb[:, (4 * cc + ss) * CW: (4 * cc + ss + 1) * CW], PV[ss][:, :]),
                waits=((pe, pe.n),))
            v_tok[4 * c + sub] = tvc
        g_PJ = (ac, tvc)
        g_PX = tvc

        for h in range(_nh_lim):
            kv = h // 4
            tws = dma('g', wstg[h % 2],
                      bass.AP(wqT, h * 128, [[NH * HD, 128], [128 * NH * HD, 32], [1, 128]]),
                      waits=((pe, g_wstg[h % 2]),))
            for d in range(NDT):
                w = [(dg, tws), g_PJ] if d == 0 else ()
                tq = mm(PJ[:, :], wstg[h % 2][:, d * 128:(d + 1) * 128],
                        xc[:, d * CW:(d + 1) * CW], start=(d == 0),
                        stop=(d == NDT - 1), waits=w)
            g_wstg[h % 2] = tq
            t_qc = sop(lambda e: e.copy(kraw, PJ[:, :]), waits=((pe, tq), (dv, g_invex)))
            t_q2 = sop(lambda e: e.activation(out=k2b, in_=kraw, func=AF.Square))
            tsum = mm(PR[0:1, :], onescol[:, :], k2b, start=True, stop=True,
                      waits=[(ac, t_q2), g_PR])
            t_ln, t_ex = rms_rows(tsum)
            g_PR = (ac, t_ln)
            tex = mm(PX[:, :], onesrow[:, :], rowb[0:1, 0:CW], start=True, stop=True,
                     waits=((ac, t_ex), (ac, g_PX)))
            g_rows_pe = tex
            t_ic = sop(lambda e: e.copy(invex, PX[:, :]), waits=((pe, tex), (dv, g_invex)))
            g_PX = t_ic
            t_qh = vop(lambda e: e.scalar_tensor_tensor(
                out=qhat, in0=PJ[:, :], scalar=qnorm[:, :], in1=invex,
                op0=ALU.mult, op1=ALU.mult), waits=((ac, t_ic), (pe, g_qhat)))
            g_invex = t_qh
            g_PJ = (dv, t_qh)
            t_qhs = settle()
            t_qn2 = sop(lambda e: e.activation(out=k2b, in_=qhat, func=AF.Square),
                        waits=((dv, t_qhs),))
            tsum = mm(PR[0:1, :], onescol[:, :], k2b, start=True, stop=True,
                      waits=[(ac, t_qn2), g_PR])
            t_sr = dma('y', srow_t[0:1, :],
                       bass.AP(srow_in, (h * NCHUNK + c) * CW, [[CW, 1], [1, CW]]),
                       waits=((dv, g_srow),))
            t_s1 = vop(lambda e, kvv=kv: e.tensor_tensor(
                out=rowf[0:1, CW:2 * CW], in0=PR[0:1, :],
                in1=maxk[0:1, kvv:kvv + 1].broadcast_to([1, CW]), op=ALU.add),
                waits=((pe, tsum),))
            vop(lambda e: e.tensor_scalar(
                out=rowf[0:1, CW:2 * CW], in0=rowf[0:1, CW:2 * CW], scalar1=-0.5,
                scalar2=None, op0=ALU.mult))
            t_s2 = vop(lambda e: e.tensor_tensor(
                out=rowf[0:1, CW:2 * CW], in0=rowf[0:1, CW:2 * CW],
                in1=srow_t[0:1, :], op=ALU.subtract), waits=((dy, t_sr),))
            t_ss = settle()
            g_srow = t_s2
            g_PR = (dv, t_s1)
            t_sb = sop(lambda e: e.copy(rowb[0:1, CW:2 * CW], rowf[0:1, CW:2 * CW]),
                       waits=((dv, t_ss), (pe, g_rows_pe)))
            g_rows_pe = pe.n

            exp_tok = [0] * KT
            for kt in range(KT):
                qs = 128 * (kt - 4 * c) if kt >= 4 * c else 0
                crossing = kt >= 4 * c
                w = [(dv, t_qhs)]
                if kt >= 2:
                    w.append((ac, exp_tok[kt - 2]))
                if kt == 0:
                    w.append((ac, t_sb))
                mm(PS[kt % 2][:, qs:CW],
                   Khat[:, kv * 2048 + kt * 128: kv * 2048 + (kt + 1) * 128],
                   qhat[:, qs:CW], start=True, stop=False, waits=w)
                ttok = mm(PS[kt % 2][:, qs:CW], onesrow[:, :],
                          rowb[0:1, CW + qs:2 * CW], start=False, stop=not crossing)
                if crossing:
                    ttok = mm(PS[kt % 2][:, qs:qs + 128], ident[:, :], trimask[:, :],
                              start=False, stop=True)
                wexp = [(pe, ttok)]
                if kt == 0:
                    wexp.append((pe, g_probs))
                exp_tok[kt] = sop(lambda e, k=kt, q=qs, hh=h: e.activation(
                    out=probs[:, k * CW + q: (k + 1) * CW], in_=PS[k % 2][:, q:CW],
                    func=AF.Exp, bias=bcol[:, hh * 16 + k: hh * 16 + k + 1], scale=RSQ),
                    waits=wexp)
            for kt in range(KT):
                qs = 128 * (kt - 4 * c) if kt >= 4 * c else 0
                w = [(ac, exp_tok[kt])]
                if kt == 0:
                    w += [g_PR, (ac, v_tok[4 * (c + 1) - 1]), (dv, g_PA)]
                mm(PR[0:1, qs:CW], onescol[:, :], probs[:, kt * CW + qs:(kt + 1) * CW],
                   start=(kt == 0), stop=(kt == KT - 1), waits=w)
                tav = mm(PA[:, qs:CW], Vb[:, kt * CW + kv * 128: kt * CW + kv * 128 + 128],
                         probs[:, kt * CW + qs:(kt + 1) * CW],
                         start=(kt == 0), stop=(kt == KT - 1))
            g_probs = tav
            t_rc = vop(lambda e: e.reciprocal(rowf[0:1, 2 * CW:3 * CW], PR[0:1, :]),
                       waits=((pe, pe.n),))
            g_PR = (dv, t_rc)
            t_rcs = settle()
            t_rb = sop(lambda e: e.copy(rowb[0:1, 2 * CW:3 * CW], rowf[0:1, 2 * CW:3 * CW]),
                       waits=((dv, t_rcs), (pe, g_rows_pe)))
            tex = mm(PX[:, :], onesrow[:, :], rowb[0:1, 2 * CW:3 * CW],
                     start=True, stop=True, waits=((ac, t_rb), (ac, g_PX)))
            g_rows_pe = tex
            t_rcp = sop(lambda e: e.copy(recb, PX[:, :]), waits=((pe, tex),))
            g_PX = t_rcp
            t_av = vop(lambda e: e.tensor_tensor(out=avb, in0=PA[:, :], in1=recb,
                                                 op=ALU.mult),
                       waits=((ac, t_rcp), (dy, g_avb), (pe, tav)))
            g_PA = t_av
            g_qhat = tav
            t_avs = settle()

            g_avb = dma('y', bass.AP(avmine, (hh_cc := h * NCHUNK + c) * AVPIECE,
                                     [[CW, 128], [1, CW]]),
                        avb, waits=((dv, t_avs),))

    def f_avx(e):
        off = goff(e, AVREGION, 0)
        return e.dma_start(out=bass.AP(avx, off, [[1, AVREGION]]),
                           in_=bass.AP(avmine, 0, [[1, AVREGION]]))
    P.e('g', f_avx, ((dy, dy.n),), (dg, 16), chain=(dg, dg.n))
    dg.n += 16
    t_avx = dg.n

    P.e('y', None, ((dy, dy.n),))
    P.e('g', None, ((dg, dg.n),))
    P.emit(nc)
    P = Plan()

    if phases == 'AB':
        P.e('g', lambda e: e.dma_start(out=out_xt.ap(), in_=bass.AP(avmine, 0, [[T, DM], [1, T]])), ((cag1, 1), (cag2, 1), (dg, dg.n)), (dg, 16))
        dg.n += 16
        P.e('y', None, ((dy, dy.n),))
        P.emit(nc)
        return nc
    vbuf = AR(0, 64 * KB, BF16)
    wostg = [AR(64 * KB + i * KB, 1 * KB, BF16) for i in range(2)]
    avjs = [AR(66 * KB + i * 2 * KB, 2 * KB, BF16) for i in range(2)]
    v2b = [AR(70 * KB + i * KB, 1 * KB, BF16) for i in range(2)]
    expx = AR(72 * KB, 2 * KB, F32)
    expy = AR(74 * KB, 2 * KB, F32)
    xob = [AR(76 * KB + i * KB, 1 * KB, BF16) for i in range(2)]
    xot = [AR(78 * KB + i * KB, 1 * KB, BF16) for i in range(2)]

    tb = vop(lambda e: e.memset(bar_sb[0:1, :], 1.0))
    tbw = dma('y', bar_i.ap(), bar_sb[0:1, :], waits=((dv, tb),))
    P.e('g', lambda e: e.collective_compute(
        "AllReduce", ALU.add, replica_groups=[[0, 1], [2, 3], [4, 5], [6, 7]],
        ins=[bar_i.ap().opt()], outs=[bar_o.ap().opt()]),
        ((dy, tbw), (dg, t_avx)), (cbar, 1))
    cbar.n += 1
    for pj in range(2):
        for cp in range(2):
            def f_gather(e, pj_=pj, cp_=cp):
                off = goff(e, 2 * AVPIECE, pj_ * AVREGION + cp_ * AVPIECE)
                src = bass.AP(avx, off, [[4 * AVPIECE, 16], [CW, 128], [1, CW]])
                dst = bass.AP(avloc, pj_ * 16 * 2 * AVPIECE + cp_ * AVPIECE,
                              [[2 * AVPIECE, 16], [CW, 128], [1, CW]])
                return e.dma_start(out=dst, in_=src)
            P.e('g', f_gather, ((cbar, 1),), (dg, 16), chain=(dg, dg.n))
            dg.n += 16
    t_avloc = dg.n

    g_stg = [0, 0]
    g_avj = [0, 0]
    ev_dv = 0
    tkn = 0
    for cs in range(8):
        ev_prev = ev_dv
        for j in range(NDT):
            tws = dma('g', wostg[j % 2],
                      bass.AP(woT, j * 128 * DM + cs * 512, [[DM, 128], [1, 512]]),
                      waits=((pe, g_stg[j % 2]),))

            def f_avr(e, jj=j, buf=avjs[j % 2]):
                return e.dma_start(
                    out=buf, in_=bass.AP(avloc, jj * 2 * AVPIECE,
                                         [[CW, 128], [AVPIECE, 2], [1, CW]]))
            P.e('g', f_avr, ((dg, t_avloc), (pe, g_avj[j % 2])), (dg, 16),
                chain=(dg, dg.n))
            dg.n += 16
            tavr = dg.n
            for ct in range(4):
                for th in range(2):
                    w = []
                    if ct == 0 and th == 0:
                        w = [(dg, tws), (dg, tavr)]
                        if j == 0:
                            w.append((dv, ev_prev))
                    tkn = mm(pbk[ct * 2 + th][:, :], wostg[j % 2][:, ct * 128:(ct + 1) * 128],
                             avjs[j % 2][:, th * CW:(th + 1) * CW],
                             start=(j == 0), stop=(j == NDT - 1),
                             waits=w, tok=(ct == 3 and th == 1))
            g_stg[j % 2] = tkn
            g_avj[j % 2] = tkn
        for ct in range(4):
            ctg = cs * 4 + ct
            for th in range(2):
                ev_dv = vop(lambda e, a=pbk[ct * 2 + th], cg=ctg, t=th: e.tensor_tensor(
                    out=vbuf[:, cg * T + t * CW: cg * T + (t + 1) * CW],
                    in0=a[:, :], in1=residual[:, cg * T + t * CW: cg * T + (t + 1) * CW],
                    op=ALU.add), waits=((pe, tkn),))

    ev_s = settle()
    sq_tok = [0, 0]
    for ctg in range(NDT):
        for th in range(2):
            w = []
            if ctg == 0:
                w = [(dv, ev_s)]
            else:
                w = [(pe, sq_tok[th])]
            t_v2 = sop(lambda e, cg=ctg, t=th, b=v2b[th]: e.activation(
                out=b, in_=vbuf[:, cg * T + t * CW: cg * T + (t + 1) * CW],
                func=AF.Square), waits=w)
            mm(pbk[0 + th][0:1, :], onescol[:, :],
               vbuf[:, ctg * T + th * CW: ctg * T + (th + 1) * CW],
               start=(ctg == 0), stop=(ctg == NDT - 1),
               waits=((dv, ev_s),) if ctg == 0 and th == 0 else ())
            sq_tok[th] = mm(pbk[2 + th][0:1, :], onescol[:, :], v2b[th],
                            start=(ctg == 0), stop=(ctg == NDT - 1),
                            waits=((ac, t_v2),), tok=True)
    t_stats = pe.n

    g_rbc = 0
    g_rowdma = 0
    t_xo = [0, 0]
    for th in range(2):
        tm = vop(lambda e, t=th: e.tensor_scalar(
            out=rowf[0:1, 0:CW], in0=pbk[t][0:1, :], scalar1=1.0 / DM, scalar2=None,
            op0=ALU.mult), waits=((pe, t_stats), (ac, ac.n), (dy, g_rowdma)))
        vop(lambda e, t=th: e.tensor_scalar(
            out=rowf[0:1, CW:2 * CW], in0=pbk[2 + t][0:1, :], scalar1=1.0 / DM,
            scalar2=None, op0=ALU.mult))
        vop(lambda e: e.tensor_tensor(out=rowf[0:1, 2 * CW:3 * CW], in0=rowf[0:1, 0:CW],
                                      in1=rowf[0:1, 0:CW], op=ALU.mult))
        t_var = vop(lambda e: e.tensor_tensor(
            out=rowf[0:1, CW:2 * CW], in0=rowf[0:1, CW:2 * CW],
            in1=rowf[0:1, 2 * CW:3 * CW], op=ALU.subtract))
        t_vars = settle()
        sop(lambda e: e.activation(out=rowf[0:1, 2 * CW:3 * CW], in_=rowf[0:1, CW:2 * CW],
                                   func=AF.Ln, bias=eps5[0:1, 0:1], scale=1.0), waits=((dv, t_vars),))
        t_rs = sop(lambda e: e.activation(out=rowf[0:1, 3 * CW:4 * CW],
                                          in_=rowf[0:1, 2 * CW:3 * CW],
                                          func=AF.Exp, scale=-0.5))
        t_mr = vop(lambda e: e.tensor_tensor(out=rowf[0:1, CW:2 * CW], in0=rowf[0:1, 0:CW],
                                             in1=rowf[0:1, 3 * CW:4 * CW], op=ALU.mult),
                   waits=((ac, t_rs),))
        t_mrs = settle()
        t_w1 = dma('y', bass.AP(rscr, 0, [[CW, 1], [1, CW]]),
                   rowf[0:1, 3 * CW:4 * CW], waits=((ac, t_rs), (dg, g_rbc), (dv, t_mrs)))
        t_w2 = dma('y', bass.AP(rscr, CW, [[CW, 1], [1, CW]]), rowf[0:1, CW:2 * CW])
        tb1 = dma('g', expx, bass.AP(rscr, 0, [[0, 128], [1, CW]]),
                  waits=((dy, t_w2), (dv, dv.n)))
        tb2 = dma('g', expy, bass.AP(rscr, CW, [[0, 128], [1, CW]]))
        g_rbc = tb2
        g_rowdma = t_w2
        for ct in range(NDT):
            w = [(dg, tb2)]
            if ct >= 2:
                w.append((dy, t_xo[ct % 2]))
            t_m1 = vop(lambda e, cg=ct, t=th, b=xob[ct % 2]: e.tensor_tensor(
                out=b, in0=vbuf[:, cg * T + t * CW: cg * T + (t + 1) * CW],
                in1=expx, op=ALU.mult), waits=w)
            t_m2 = vop(lambda e, b1=xob[ct % 2], b2=xot[ct % 2]: e.tensor_tensor(
                out=b2, in0=b1, in1=expy, op=ALU.subtract))
            t_m2s = settle()
            t_xo[ct % 2] = dma('y', bass.AP(x2d, ct * 128 * T + th * CW,
                                            [[T, 128], [1, CW]]),
                               xot[ct % 2], waits=((dv, t_m2s),))
    P.e('y', None, ((dy, dy.n),))
    P.e('g', None, ((dg, dg.n),))
    P.emit(nc)
    P = Plan()

    if phases == 'ABC':
        P.e('g', lambda e: e.dma_start(out=out_xt.ap(), in_=bass.AP(x2d, 0, [[1, DM * T]])), ((cag1, 1), (cag2, 1), (dy, dy.n)), (dg, 16))
        dg.n += 16
        P.e('y', None, ((dy, dy.n),))
        P.emit(nc)
        return nc
    x2tc = AR(0, 32 * KB, BF16)
    hbuf = AR(32 * KB, 86 * KB, BF16)
    pstg = [RS(0 + i * 16 * KB, 16 * KB, BF16) for i in range(2)]
    vbufD = RS(0, 32 * KB, BF16)
    wdstg = [RS(32 * KB + i * KB, 1 * KB, BF16) for i in range(2)]
    sgb = [RS(34 * KB + i * KB, 1 * KB, BF16) for i in range(2)]
    vb2d = [RS(36 * KB + i * KB, 1 * KB, BF16) for i in range(2)]
    expxD = RS(38 * KB, 2 * KB, F32)
    expyD = RS(40 * KB, 2 * KB, F32)
    oflD = [RS(42 * KB + i * 2 * KB, 2 * KB, F32) for i in range(2)]
    oflD2 = [RS(46 * KB + i * 2 * KB, 2 * KB, F32) for i in range(2)]

    g_pe_tc = 0
    g_rbcD = 0
    g_rowdmaD = 0
    for tc in range(2):
        t_x2 = dma('y', x2tc, bass.AP(x2d, tc * CW, [[T, 128], [128 * T, NDT], [1, CW]]),
                   waits=((pe, g_pe_tc),))

        PG = [pbk[0], pbk[1]]
        PU = [pbk[2], pbk[3]]
        sil_tok = [0] * NPAIR
        h_tok = [0] * NPAIR
        pe_pair = [g_pe_tc, g_pe_tc]
        for p in range(NPAIR):
            wmn = [(pe, pe_pair[p % 2])]
            if tc == 1 and p < 2:
                wmn.append((dy, g_ofl_dy))
                wmn.append((dv, g_vnorm))
            tmn = dma('y', pstg[p % 2],
                      bass.AP(ggu, p * GROW, [[256, 128], [128 * 256, 32], [1, 256]]),
                      waits=wmn)
            for d in range(NDT):
                w = []
                if d == 0:
                    w = [(dy, tmn), (dy, t_x2)]
                    if p == 0:
                        w.append((cag1, 1))
                        w.append((cag2, 1))
                    if p >= 2:
                        w.append((ac, sil_tok[p - 2]))
                mm(PG[p % 2][:, :], pstg[p % 2][:, d * 256: d * 256 + 128],
                   x2tc[:, d * CW:(d + 1) * CW], start=(d == 0), stop=(d == NDT - 1),
                   waits=w)
            for d in range(NDT):
                w = []
                if d == 0 and p >= 2:
                    w = [(dv, h_tok[p - 2])]
                tup = mm(PU[p % 2][:, :], pstg[p % 2][:, d * 256 + 128: d * 256 + 256],
                         x2tc[:, d * CW:(d + 1) * CW], start=(d == 0),
                         stop=(d == NDT - 1), waits=w)
            pe_pair[p % 2] = tup
            sil_tok[p] = sop(lambda e, pp=p: e.activation(
                out=sgb[pp % 2], in_=PG[pp % 2][:, :], func=AF.Silu),
                waits=((pe, tup),))
            h_tok[p] = vop(lambda e, pp=p: e.tensor_tensor(
                out=hbuf[:, pp * CW:(pp + 1) * CW], in0=sgb[pp % 2],
                in1=PU[pp % 2][:, :], op=ALU.mult), waits=((ac, sil_tok[p]),))
        t_hs = settle()

        PD = [pbk[0], pbk[1], pbk[2], pbk[3]]
        g_wd = [0, 0]
        ev2 = 0
        for cs in range(8):
            ev_p = ev2
            for p in range(NPAIR):
                w = [(pe, g_wd[p % 2])]
                if cs == 0 and p < 2:
                    w.append((dv, t_hs))
                twd = dma('y', wdstg[p % 2],
                          bass.AP(gd, p * DROW + cs * 512, [[4096, 128], [1, 512]]),
                          waits=w)
                for j in range(4):
                    wj = []
                    if j == 0:
                        wj = [(dy, twd)]
                        if p == 0:
                            wj += [(dv, t_hs), (dv, ev_p)]
                    tdn = mm(PD[j][:, :], wdstg[p % 2][:, j * 128:(j + 1) * 128],
                             hbuf[:, p * CW:(p + 1) * CW], start=(p == 0),
                             stop=(p == NPAIR - 1), waits=wj, tok=(j == 3))
                g_wd[p % 2] = tdn
            for j in range(4):
                ctg = cs * 4 + j
                ev2 = vop(lambda e, a=PD[j], cg=ctg: e.tensor_tensor(
                    out=vbufD[:, cg * CW:(cg + 1) * CW], in0=a[:, :],
                    in1=x2tc[:, cg * CW:(cg + 1) * CW], op=ALU.add),
                    waits=((pe, tdn),))

        ev2s = settle()
        sq2 = 0
        for ctg in range(NDT):
            t_v2 = sop(lambda e, cg=ctg, b=vb2d[ctg % 2]: e.activation(
                out=b, in_=vbufD[:, cg * CW:(cg + 1) * CW], func=AF.Square),
                waits=((dv, ev2s),) if ctg == 0 else ((pe, sq2),))
            mm(pbk[4][0:1, :], onescol[:, :], vbufD[:, ctg * CW:(ctg + 1) * CW],
               start=(ctg == 0), stop=(ctg == NDT - 1),
               waits=((dv, ev2s),) if ctg == 0 else ())
            sq2 = mm(pbk[5][0:1, :], onescol[:, :], vb2d[ctg % 2],
                     start=(ctg == 0), stop=(ctg == NDT - 1),
                     waits=((ac, t_v2),), tok=True)
        t_st2 = pe.n

        tm = vop(lambda e: e.tensor_scalar(
            out=rowf[0:1, 0:CW], in0=pbk[4][0:1, :], scalar1=1.0 / DM, scalar2=None,
            op0=ALU.mult), waits=((pe, t_st2), (ac, ac.n), (dy, g_rowdmaD)))
        vop(lambda e: e.tensor_scalar(
            out=rowf[0:1, CW:2 * CW], in0=pbk[5][0:1, :], scalar1=1.0 / DM, scalar2=None,
            op0=ALU.mult))
        vop(lambda e: e.tensor_tensor(out=rowf[0:1, 2 * CW:3 * CW], in0=rowf[0:1, 0:CW],
                                      in1=rowf[0:1, 0:CW], op=ALU.mult))
        t_var = vop(lambda e: e.tensor_tensor(
            out=rowf[0:1, CW:2 * CW], in0=rowf[0:1, CW:2 * CW],
            in1=rowf[0:1, 2 * CW:3 * CW], op=ALU.subtract))
        t_vars = settle()
        sop(lambda e: e.activation(out=rowf[0:1, 2 * CW:3 * CW], in_=rowf[0:1, CW:2 * CW],
                                   func=AF.Ln, bias=eps5[0:1, 0:1], scale=1.0), waits=((dv, t_vars),))
        t_rs = sop(lambda e: e.activation(out=rowf[0:1, 3 * CW:4 * CW],
                                          in_=rowf[0:1, 2 * CW:3 * CW],
                                          func=AF.Exp, scale=-0.5))
        t_mr = vop(lambda e: e.tensor_tensor(out=rowf[0:1, CW:2 * CW], in0=rowf[0:1, 0:CW],
                                             in1=rowf[0:1, 3 * CW:4 * CW], op=ALU.mult),
                   waits=((ac, t_rs),))
        t_mrs = settle()
        t_w1 = dma('y', bass.AP(rscr, 0, [[CW, 1], [1, CW]]),
                   rowf[0:1, 3 * CW:4 * CW], waits=((ac, t_rs), (dg, g_rbcD), (dv, t_mrs)))
        t_w2 = dma('y', bass.AP(rscr, CW, [[CW, 1], [1, CW]]), rowf[0:1, CW:2 * CW])
        tb1 = dma('g', expxD, bass.AP(rscr, 0, [[0, 128], [1, CW]]),
                  waits=((dy, t_w2), (dv, dv.n)))
        tb2 = dma('g', expyD, bass.AP(rscr, CW, [[0, 128], [1, CW]]))
        g_rbcD = tb2
        g_rowdmaD = t_w2
        t_ot = [0, 0]
        for ct in range(NDT):
            w = [(dg, tb2)]
            if ct >= 2:
                w.append((dy, t_ot[ct % 2]))
            t_m1 = vop(lambda e, cg=ct, b=oflD[ct % 2]: e.tensor_tensor(
                out=b, in0=vbufD[:, cg * CW:(cg + 1) * CW], in1=expxD, op=ALU.mult),
                waits=w)
            t_m2 = vop(lambda e, b=oflD[ct % 2], b2=oflD2[ct % 2]: e.tensor_tensor(
                out=b2, in0=b, in1=expyD, op=ALU.subtract))
            t_m2s = settle()
            t_ot[ct % 2] = dma('y', bass.AP(out_xt, ct * 128 * T + tc * CW,
                                            [[T, 128], [1, CW]]),
                               oflD2[ct % 2], waits=((dv, t_m2s),))
        g_pe_tc = pe.n
        g_ofl_dy = dy.n
        g_vnorm = dv.n

    P.e('y', None, ((dy, dy.n),))
    P.e('g', None, ((dg, dg.n),))
    P.emit(nc)
    return nc


def prepare_in_maps(L0_attn_x, L0_attn_Wq, L0_attn_Wk, L0_attn_Wv, L0_attn_Wo,
                    L0_attn_q_norm_w, L0_attn_k_norm_w,
                    L0_ffn_qweight_gu, L0_ffn_qzeros_gu, L0_ffn_scales_gu,
                    L0_ffn_g_idx_gu, L0_ffn_qweight_down, L0_ffn_qzeros_down,
                    L0_ffn_scales_down, L0_ffn_g_idx_down):
    x = np.ascontiguousarray(np.asarray(L0_attn_x, dtype=np.float32))
    Wq = np.asarray(L0_attn_Wq, dtype=np.float32)
    Wk = np.asarray(L0_attn_Wk, dtype=np.float32)
    Wv = np.asarray(L0_attn_Wv, dtype=np.float32)
    Wo = np.asarray(L0_attn_Wo, dtype=np.float32)
    qn = np.asarray(L0_attn_q_norm_w, dtype=np.float32)
    kn = np.asarray(L0_attn_k_norm_w, dtype=np.float32)
    qwg = np.ascontiguousarray(np.asarray(L0_ffn_qweight_gu))
    qzg = np.ascontiguousarray(np.asarray(L0_ffn_qzeros_gu))
    scg = np.ascontiguousarray(np.asarray(L0_ffn_scales_gu, dtype=np.float32))
    qwd = np.ascontiguousarray(np.asarray(L0_ffn_qweight_down))
    qzd = np.ascontiguousarray(np.asarray(L0_ffn_qzeros_down))
    scd = np.ascontiguousarray(np.asarray(L0_ffn_scales_down, dtype=np.float32))
    assert np.array_equal(np.asarray(L0_ffn_g_idx_gu), np.arange(4096) // 128)
    assert np.array_equal(np.asarray(L0_ffn_g_idx_down), np.arange(11008) // 128)

    wqT = np.ascontiguousarray(Wq.T)
    wkT = np.ascontiguousarray(Wk.T)
    wvT = np.ascontiguousarray(Wv.T)
    woT = np.ascontiguousarray(Wo.T)

    slopes = _slopes()
    iotac = np.arange(128, dtype=np.float32)
    trimask = np.where(iotac[:, None] > iotac[None, :], np.float32(NEG),
                       np.float32(0.0)).astype(ml_dtypes.bfloat16)
    ident128 = np.eye(128, dtype=ml_dtypes.bfloat16)
    onesrow = np.ones((1, 128), dtype=ml_dtypes.bfloat16)
    onescol = np.ones((128, 1), dtype=ml_dtypes.bfloat16)
    _sh = (np.arange(128) % 8) * 4
    maskcol = (np.uint32(15) << _sh.astype(np.uint32)).reshape(128, 1)
    pw2col = (2.0 ** (-_sh.astype(np.float64))).astype(np.float32).reshape(128, 1)

    units = [[p for p in range(NPAIR) if p // UPC == r] for r in range(8)]

    in_maps = []
    for core in range(8):
        b = core // 2
        par = core % 2
        h0 = 16 * par
        kv0 = 4 * par
        xb = np.ascontiguousarray(x[b].T)

        srow = np.zeros((NH * NCHUNK, CW), dtype=np.float32)
        bcol = np.zeros((128, NH * 16), dtype=np.float32)
        for hh in range(NH):
            sl = float(slopes[h0 + hh])
            for c in range(NCHUNK):
                srow[hh * NCHUNK + c, :] = sl * (c * CW + np.arange(CW, dtype=np.float32)) * np.float32(math.sqrt(128.0))
            for kt in range(16):
                bcol[:, hh * 16 + kt] = sl * (128 * kt + np.arange(128, dtype=np.float32))

        ups = units[core]
        qw_gu_t = np.zeros((512, 256 * UPC), dtype=np.uint32)
        sc_gu_s = np.zeros((32, 256 * UPC), dtype=np.float32)
        qz_gu_s = np.zeros((32, 32 * UPC), dtype=np.uint32)
        qw_d_t = np.zeros((16 * UPC, 4096), dtype=np.uint32)
        sc_d_s = np.zeros((UPC, 4096), dtype=np.float32)
        qz_d_s = np.zeros((UPC, 512), dtype=np.uint32)
        for i, pp in enumerate(ups):
            gc = slice(128 * pp, 128 * pp + 128)
            uc = slice(128 * (pp + NPAIR), 128 * (pp + NPAIR) + 128)
            qw_gu_t[:, 256 * i:256 * i + 128] = qwg[:, gc]
            qw_gu_t[:, 256 * i + 128:256 * i + 256] = qwg[:, uc]
            sc_gu_s[:, 256 * i:256 * i + 128] = scg[:, gc]
            sc_gu_s[:, 256 * i + 128:256 * i + 256] = scg[:, uc]
            gw = slice(16 * pp, 16 * pp + 16)
            uw = slice(16 * (pp + NPAIR), 16 * (pp + NPAIR) + 16)
            qz_gu_s[:, 32 * i:32 * i + 16] = qzg[:, gw]
            qz_gu_s[:, 32 * i + 16:32 * i + 32] = qzg[:, uw]
            qw_d_t[16 * i:16 * i + 16, :] = qwd[16 * pp:16 * pp + 16, :]
            sc_d_s[i, :] = scd[pp, :]
            qz_d_s[i, :] = qzd[pp, :]
        qw_gu_s = np.ascontiguousarray(np.repeat(qw_gu_t, 8, axis=0))
        qw_d_s = np.ascontiguousarray(np.repeat(qw_d_t, 8, axis=0))

        in_maps.append(dict(
            xT_b=xb,
            xown=np.ascontiguousarray(xb[:, par * T:(par + 1) * T]),
            wqT_s=np.ascontiguousarray(wqT[:, h0 * 128:(h0 + NH) * 128]),
            wkT_s=np.ascontiguousarray(wkT[:, kv0 * 128:(kv0 + NKV) * 128]),
            wvT_s=np.ascontiguousarray(wvT[:, kv0 * 128:(kv0 + NKV) * 128]),
            woT=woT,
            qnorm=qn.reshape(128, 1).astype(np.float32),
            knorm=kn.reshape(128, 1).astype(np.float32),
            srow_in=srow, bcol_in=bcol, trimask=trimask, ident128=ident128,
            onesrow=onesrow, onescol=onescol, maskcol=maskcol, pw2col=pw2col,
            qw_gu_s=qw_gu_s, sc_gu_s=sc_gu_s, qz_gu_s=qz_gu_s,
            qw_d_s=qw_d_s, sc_d_s=sc_d_s, qz_d_s=qz_d_s,
        ))

    return in_maps


def assemble_out(results):
    out = np.empty((B, S, DM), dtype=np.float32)
    for core in range(8):
        b = core // 2
        par = core % 2
        o = np.asarray(results[core]["out_xt"], dtype=np.float32)
        out[b, par * T:(par + 1) * T, :] = o.T
    return out


def kernel(**inputs):
    in_maps = prepare_in_maps(**inputs)
    nc = build_program()
    res = run_bass_kernel_spmd(nc, in_maps, core_ids=list(range(8)))
    return assemble_out(res.results)


# version 07 (final)
# speedup vs baseline: 1.0537x; 1.0537x over previous
import math
import numpy as np
import ml_dtypes
import concourse.bass as bass
import concourse.mybir as mybir
from concourse.bass_utils import run_bass_kernel_spmd

F32 = mybir.dt.float32
BF16 = mybir.dt.bfloat16
U32 = mybir.dt.uint32
I32 = mybir.dt.int32
U8 = mybir.dt.uint8
AF = mybir.ActivationFunctionType
ALU = mybir.AluOpType
AXX = mybir.AxisListType.X

DM, NDT, S, B = 4096, 32, 2048, 4
NH, NKV, HD = 16, 4, 128
NCHUNK, CW, T = 4, 512, 1024
NPAIR, UPC = 86, 11
GROW = 32 * 128 * 256
DROW = 128 * 4096
RSQ = 1.0 / math.sqrt(128.0)
NEG = -1.0e30
KB = 1024
AVPIECE = 128 * CW
AVREGION = NH * NCHUNK * AVPIECE


def _slopes():
    start = 2 ** (-(2 ** -(math.log2(32) - 3)))
    return np.array([start * start ** i for i in range(32)], dtype=np.float32)


class Ch:
    def __init__(self, nc, name):
        self.h = nc.alloc_semaphore(name)
        self.n = 0


class Plan:
    def __init__(self):
        self.ev = []

    def e(self, eng, fn=None, waits=(), inc=None, chain=None):
        self.ev.append((eng, fn, list(waits), inc, chain))

    def run(self, name, eng):
        for (engine, fn, waits, inc, chain) in self.ev:
            if engine != name:
                continue
            if chain is not None and chain[1] > 0:
                eng.wait_ge(chain[0].h, chain[1])
            for (ch, val) in waits:
                if val > 0:
                    eng.wait_ge(ch.h, val)
            if fn is not None:
                inst = fn(eng)
                if inc is not None:
                    ch, amt = inc
                    inst.then_inc(ch.h, amt)

    def emit(self, nc):
        with nc.Block() as blk:
            @blk.sync
            def _(e):
                self.run('y', e)

            @blk.gpsimd
            def _(e):
                self.run('g', e)

            @blk.tensor
            def _(e):
                self.run('t', e)

            @blk.vector
            def _(e):
                self.run('v', e)

            @blk.scalar
            def _(e):
                self.run('s', e)


def build_program(phases='ABCD', detect_races=True):
    nc = bass.Bass("TRN2", target_bir_lowering=False, detect_race_conditions=detect_races)
    nc.cache_partition_id()

    def din(name, shape, dt):
        return nc.dram_tensor(name, shape, dt, kind="ExternalInput")

    xT = din("xT_b", [DM, S], F32)
    xown = din("xown", [DM, T], F32)
    wqT = din("wqT_s", [DM, NH * HD], F32)
    wkT = din("wkT_s", [DM, NKV * HD], F32)
    wvT = din("wvT_s", [DM, NKV * HD], F32)
    woT = din("woT", [DM, DM], F32)
    qnorm_i = din("qnorm", [128, 1], F32)
    knorm_i = din("knorm", [128, 1], F32)
    srow_in = din("srow_in", [NH * NCHUNK, CW], F32)
    bcol_in = din("bcol_in", [128, NH * 16], F32)
    trimask_i = din("trimask", [128, 128], BF16)
    ident_i = din("ident128", [128, 128], BF16)
    onesrow_i = din("onesrow", [1, 128], BF16)
    onescol_i = din("onescol", [128, 1], BF16)
    maskcol_i = din("maskcol", [128, 1], U32)
    pw2col_i = din("pw2col", [128, 1], F32)
    qw_gu = din("qw_gu_s", [4096, 256 * UPC], U32)
    sc_gu = din("sc_gu_s", [32, 256 * UPC], F32)
    qz_gu = din("qz_gu_s", [32, 32 * UPC], U32)
    qw_d = din("qw_d_s", [128 * UPC, 4096], U32)
    sc_d = din("sc_d_s", [UPC, 4096], F32)
    qz_d = din("qz_d_s", [UPC, 512], U32)

    out_xt = nc.dram_tensor("out_xt", [DM, T], F32, kind="ExternalOutput")
    if 'DBG' in phases:
        dbg_out = nc.dram_tensor("dbg_out", [128, 16 * CW], F32, kind="ExternalOutput")

    shard_gu = nc.dram_tensor("shard_gu", [UPC, GROW], BF16)
    ggu = nc.dram_tensor("ggu", [8 * UPC, GROW], BF16)
    shard_d = nc.dram_tensor("shard_d", [UPC, DROW], BF16)
    gd = nc.dram_tensor("gd", [8 * UPC, DROW], BF16)
    avx = nc.dram_tensor("avx", [2, AVREGION], BF16, addr_space="Shared")
    avmine = nc.dram_tensor("avmine", [1, AVREGION], BF16)
    avloc = nc.dram_tensor("avloc", [1, 32 * 2 * AVPIECE], BF16)
    x2d = nc.dram_tensor("x2d", [DM, T], BF16)
    zscr = nc.dram_tensor("zscr", [1, 8192], F32)
    rscr = nc.dram_tensor("rscr", [1, 2048], F32)
    bar_i = nc.dram_tensor("bar_i", [1, 16], F32)
    bar_o = nc.dram_tensor("bar_o", [1, 16], F32)

    qnorm = nc.alloc_sbuf_tensor("qnorm_sb", [128, 1], F32)
    knorm = nc.alloc_sbuf_tensor("knorm_sb", [128, 1], F32)
    bcol = nc.alloc_sbuf_tensor("bcol_sb", [128, NH * 16], F32)
    trimask = nc.alloc_sbuf_tensor("trimask_sb", [128, 128], BF16)
    ident = nc.alloc_sbuf_tensor("ident_sb", [128, 128], BF16)
    junk = nc.alloc_sbuf_tensor("junk_sb", [1, 16], F32)
    onesrow = nc.alloc_sbuf_tensor("onesrow_sb", [1, 128], BF16)
    onescol = nc.alloc_sbuf_tensor("onescol_sb", [128, 1], BF16)
    maskcol = nc.alloc_sbuf_tensor("maskcol_sb", [128, 1], U32)
    pw2col = nc.alloc_sbuf_tensor("pw2col_sb", [128, 1], F32)
    srow_t = nc.alloc_sbuf_tensor("srow_t", [1, CW], F32)
    rowf = nc.alloc_sbuf_tensor("rowf_sb", [1, 2048], F32)
    rowb = nc.alloc_sbuf_tensor("rowb_sb", [1, 1536], BF16)
    maxk = nc.alloc_sbuf_tensor("maxk_sb", [1, 8], F32)
    bar_sb = nc.alloc_sbuf_tensor("bar_sb", [1, 16], F32)
    eps6 = nc.alloc_sbuf_tensor("eps6_sb", [1, 1], F32)
    eps5 = nc.alloc_sbuf_tensor("eps5_sb", [1, 1], F32)
    residual = nc.alloc_sbuf_tensor("residual", [128, 32 * T], BF16)

    ARENA = 118 * KB
    arena = nc.alloc_sbuf_tensor("arena", [128, ARENA], U8)

    def AR(off, nbytes, dt, parts=128):
        assert off + nbytes <= ARENA, (off, nbytes)
        return arena[0:parts, off:off + nbytes].bitcast(dt)

    def RS(off, nbytes, dt, parts=128):
        assert off + nbytes <= 64 * KB, (off, nbytes)
        ap = residual[0:parts, off // 2:(off + nbytes) // 2]
        return ap if dt == BF16 else ap.bitcast(dt)

    pbk = [nc.alloc_psum_tensor(f"pb{i}", [128, 512], F32) for i in range(8)]

    dg = Ch(nc, "dg")
    dy = Ch(nc, "dy")
    cag1 = Ch(nc, "cag1")
    cag2 = Ch(nc, "cag2")
    cbar = Ch(nc, "cbar")
    pe = Ch(nc, "pe")
    dv = Ch(nc, "dv")
    ac = Ch(nc, "ac")

    P = Plan()
    _regs = {}

    def goff(e, mul, const):
        if 'g' not in _regs:
            rpar = e.to_reg(e.partition_id() % 2)
            roff = e.alloc_register('roff')
            _regs['g'] = (rpar, roff)
        rpar, roff = _regs['g']
        e.reg_mul(roff, rpar, mul)
        e.reg_add(roff, roff, const)
        return roff

    def dma(q, out, in_, waits=()):
        ch = dg if q == 'g' else dy
        P.e(q, lambda e, o=out, i=in_: e.dma_start(out=o, in_=i), waits, (ch, 16),
            chain=(ch, ch.n))
        ch.n += 16
        return ch.n

    def mm(out, lhsT, rhs, start, stop, waits=(), tok=False):
        do_inc = stop or tok
        def f(e, o=out, l=lhsT, r=rhs, st=start, sp=stop):
            return e.matmul(o, l, r, start=st, stop=sp)
        if do_inc:
            pe.n += 1
        P.e('t', f, waits, (pe, 1) if do_inc else None)
        return pe.n

    def vop(fn, waits=()):
        dv.n += 1
        P.e('v', fn, waits, (dv, 1), chain=(dv, dv.n - 1))
        return dv.n

    def sop(fn, waits=()):
        ac.n += 1
        P.e('s', fn, waits, (ac, 1), chain=(ac, ac.n - 1))
        return ac.n

    def settle():
        return vop(lambda e: e.memset(junk[0:1, :], 0.0))

    if 'NOCONSTS' not in phases:
        for (dst, src) in [
            (qnorm[:, :], qnorm_i.ap()), (knorm[:, :], knorm_i.ap()),
            (bcol[:, :], bcol_in.ap()), (trimask[:, :], trimask_i.ap()),
            (onesrow[:, :], onesrow_i.ap()), (onescol[:, :], onescol_i.ap()),
            (maskcol[:, :], maskcol_i.ap()), (pw2col[:, :], pw2col_i.ap()),
        (ident[:, :], ident_i.ap()),
        ]:
            dma('y', dst, src)
    t_consts = dy.n
    if 'NOEPS' not in phases:
        vop(lambda e: e.memset(eps6[:, :], 1e-6))
        vop(lambda e: e.memset(eps5[:, :], 1e-5))

    if 'NORES' not in phases:
        dma('g', residual[:, :], bass.AP(xown, 0, [[T, 128], [128 * T, NDT], [1, T]]))

    wst = AR(0, 16 * KB, U32)
    nib = AR(16 * KB, 16 * KB, U32)
    sce = AR(32 * KB, 16 * KB, F32)
    zse = AR(48 * KB, 16 * KB, F32)
    wb1 = AR(64 * KB, 16 * KB, F32)
    wb2 = AR(80 * KB, 8 * KB, BF16)
    zqu = AR(88 * KB, 1 * KB, U32, parts=32)
    zqw = AR(89 * KB, 128, U32, parts=32)
    scu = AR(90 * KB, 1 * KB, F32, parts=32)[:, 0:256]
    tmpu = AR(91 * KB, 1 * KB, F32, parts=32)[:, 0:256]
    zsu = AR(92 * KB, 1 * KB, F32, parts=32)[:, 0:256]
    zrow_u = AR(93 * KB, 2 * KB, U32, parts=1)
    zrow_q = AR(95 * KB, 16 * KB, U32, parts=1)
    zrow_s = AR(64 * KB, 16 * KB, F32, parts=1)
    zrow_f = AR(48 * KB, 16 * KB, F32, parts=1)
    zrow_b = AR(16 * KB, 16 * KB, F32, parts=1)

    g_w_dv = 0
    g_small = 0
    g_zstore = 0
    prev_store = 0
    for u in range([] if 'NODEQ' in phases else range(UPC)) if False else (range(0) if 'NODEQ' in phases else range(UPC)):
        tz = dma('y', zqw, bass.AP(qz_gu, u * 32, [[32 * UPC, 32], [1, 32]]),
                 waits=((dv, g_small),))
        tsc = dma('g', scu, bass.AP(sc_gu, u * 256, [[256 * UPC, 32], [1, 256]]),
                  waits=((dv, g_small),))
        zq3u = zqu.rearrange("g (j s) -> g j s", s=8)
        for sft in range(8):
            vop(lambda e, s_=sft: e.tensor_scalar(
                out=zq3u[:, :, s_:s_ + 1], in0=zqw, scalar1=4 * s_, scalar2=15,
                op0=ALU.logical_shift_right, op1=ALU.bitwise_and),
                waits=((dy, tz),) if sft == 0 else ())
        vop(lambda e: e.scalar_tensor_tensor(
            out=tmpu, in0=zqu, scalar=1.0, in1=scu, op0=ALU.add, op1=ALU.mult),
            waits=((dg, tsc),))
        t_zs = vop(lambda e: e.tensor_scalar(
            out=zsu, in0=tmpu, scalar1=-1.0, scalar2=None, op0=ALU.mult),
            waits=((dy, g_zstore),))
        t_zss = settle()
        t_zw = dma('y', bass.AP(zscr, 0, [[256, 32], [1, 256]]), zsu,
                   waits=((dv, t_zss), (dg, dg.n)))

        for half in range(2):
            fc = u * 256 + half * 128
            tw = dma('y', wst,
                     bass.AP(qw_gu, fc, [[256 * UPC, 128],
                                         [128 * 256 * UPC, 32], [1, 128]]),
                     waits=((dv, g_w_dv),))
            ts = dma('g', sce, bass.AP(sc_gu, fc, [[0, 128], [256 * UPC, 32], [1, 128]]),
                     waits=((dv, g_w_dv),))
            tz2 = dma('g', zse, bass.AP(zscr, half * 128, [[0, 128], [256, 32], [1, 128]]),
                      waits=((dy, t_zw), (dv, g_w_dv)))
            vop(lambda e: e.tensor_scalar(
                out=nib, in0=wst, scalar1=maskcol[:, :], scalar2=None,
                op0=ALU.bitwise_and),
                waits=((dy, tw), (dy, t_consts)))
            vop(lambda e: e.scalar_tensor_tensor(
                out=wb1, in0=nib, scalar=pw2col[:, :], in1=sce,
                op0=ALU.mult, op1=ALU.mult), waits=((dg, ts),))
            wv = vop(lambda e: e.tensor_tensor(out=wb2, in0=wb1, in1=zse, op=ALU.add),
                     waits=((dg, tz2), (dy, prev_store)))
            g_w_dv = wv
            wvs = settle()
            prev_store = dma('y', bass.AP(shard_gu, u * GROW + half * 128,
                                          [[256, 128], [128 * 256, 32], [1, 128]]),
                             wb2, waits=((dv, wvs),))
        g_small = t_zs
        g_zstore = t_zw

        tzr = dma('y', zrow_u, bass.AP(qz_d, u * 512, [[512, 1], [1, 512]]),
                  waits=((dv, g_small),))
        tss = dma('y', zrow_s, bass.AP(sc_d, u * 4096, [[4096, 1], [1, 4096]]))
        zr3 = zrow_q.rearrange("g (j s) -> g j s", s=8)
        for sft in range(8):
            vop(lambda e, s_=sft: e.tensor_scalar(
                out=zr3[:, :, s_:s_ + 1], in0=zrow_u, scalar1=4 * s_,
                scalar2=15, op0=ALU.logical_shift_right, op1=ALU.bitwise_and),
                waits=((dy, tzr),) if sft == 0 else ())
        vop(lambda e: e.scalar_tensor_tensor(
            out=zrow_f, in0=zrow_q, scalar=1.0, in1=zrow_s,
            op0=ALU.add, op1=ALU.mult), waits=((dy, tss), (dy, prev_store)))
        t_zd = vop(lambda e: e.tensor_scalar(
            out=zrow_b, in0=zrow_f, scalar1=-1.0, scalar2=None, op0=ALU.mult))
        t_zds = settle()
        t_zw2 = dma('y', bass.AP(zscr, 0, [[4096, 1], [1, 4096]]), zrow_b,
                    waits=((dv, t_zds), (dg, dg.n)))
        g_small = t_zd
        g_zstore = t_zw2
        tw = dma('y', wst, bass.AP(qw_d, u * 128 * 4096, [[4096, 128], [1, 4096]]),
                 waits=((dv, g_w_dv),))
        ts = dma('g', sce, bass.AP(sc_d, u * 4096, [[0, 128], [1, 4096]]),
                 waits=((dv, g_w_dv),))
        tz2 = dma('g', zse, bass.AP(zscr, 0, [[0, 128], [1, 4096]]),
                  waits=((dy, t_zw2), (dv, g_w_dv)))
        vop(lambda e: e.tensor_scalar(
            out=nib, in0=wst, scalar1=maskcol[:, :], scalar2=None,
            op0=ALU.bitwise_and), waits=((dy, tw), (dy, t_zw2)))
        vop(lambda e: e.scalar_tensor_tensor(
            out=wb1, in0=nib, scalar=pw2col[:, :], in1=sce,
            op0=ALU.mult, op1=ALU.mult), waits=((dg, ts),))
        wv = vop(lambda e: e.tensor_tensor(out=wb2, in0=wb1, in1=zse, op=ALU.add),
                 waits=((dg, tz2), (dy, prev_store)))
        g_w_dv = wv
        wvs = settle()
        prev_store = dma('y', bass.AP(shard_d, u * DROW, [[4096, 128], [1, 4096]]),
                         wb2, waits=((dv, wvs),))
    t_shards = dy.n

    P.e('g', lambda e: e.collective_compute(
        "AllGather", ALU.bypass, replica_groups=[list(range(8))],
        ins=[shard_gu.ap().opt()], outs=[ggu.ap().opt()]),
        ((dy, t_shards),), (cag1, 1))
    cag1.n += 1
    P.e('g', lambda e: e.collective_compute(
        "AllGather", ALU.bypass, replica_groups=[list(range(8))],
        ins=[shard_d.ap().opt()], outs=[gd.ap().opt()]),
        (), (cag2, 1))
    cag2.n += 1

    P.e('y', None, ((dy, dy.n),))
    P.e('g', None, ((dg, dg.n), (dv, dv.n)))
    P.emit(nc)
    P = Plan()

    if phases.startswith('A') and 'B' not in phases:
        P.e('g', lambda e: e.dma_start(out=out_xt.ap(), in_=bass.AP(ggu, 0, [[T, DM], [1, T]])), ((cag1, 1), (cag2, 1)), (dg, 16))
        dg.n += 16
        P.e('y', None, ((dy, dy.n),))
        P.emit(nc)
        return nc
    Khat = AR(0, 16 * KB, BF16)
    Vb = AR(16 * KB, 16 * KB, BF16)
    xc = AR(32 * KB, 32 * KB, BF16)
    wstg = [AR(64 * KB + i * 8 * KB, 8 * KB, BF16) for i in range(2)]
    probs = AR(80 * KB, 16 * KB, BF16)
    s1b = [AR(96 * KB + i * 2 * KB, 2 * KB, F32) for i in range(2)]
    shex = AR(100 * KB, 2 * KB, F32)
    invex = AR(102 * KB, 2 * KB, F32)
    kraw = AR(104 * KB, 1 * KB, BF16)
    k2b = AR(105 * KB, 1 * KB, BF16)
    qhat = AR(106 * KB, 1 * KB, BF16)
    avb = AR(107 * KB, 1 * KB, BF16)
    wvstg = [AR(108 * KB + i * KB, 1 * KB, BF16) for i in range(2)]
    recb = AR(110 * KB, 1 * KB, BF16)

    PJ, PX = pbk[0], pbk[1]
    PV = [pbk[0], pbk[1], pbk[2], pbk[3]]
    PS = [pbk[4], pbk[5]]
    PA = pbk[6]
    PR = pbk[7]

    v_tok = {}
    g_wstg = [0, 0]
    g_shex = 0
    g_invex = 0
    g_qhat = 0
    g_probs = 0
    g_avb = 0
    g_PR = (dv, 0)
    g_PA = 0
    g_PJ = (ac, 0)
    g_PX = 0
    g_rows_pe = 0
    g_srow = 0
    v_stage = [0, 0]

    def rms_rows(tsum):
        t_ln = sop(lambda e: e.activation(out=rowf[0:1, 0:CW], in_=PR[0:1, :],
                                          func=AF.Ln, bias=eps6[0:1, 0:1], scale=1.0 / 128.0),
                   waits=((pe, tsum), (pe, g_rows_pe)))
        t_ex = sop(lambda e: e.activation(out=rowb[0:1, 0:CW], in_=rowf[0:1, 0:CW],
                                          func=AF.Exp, scale=-0.5))
        return t_ln, t_ex

    import re as _re
    _mc = _re.search(r'C(\d+)', phases)
    _mh = _re.search(r'H(\d+)', phases)
    _nc_lim = int(_mc.group(1)) if _mc else NCHUNK
    _nh_lim = int(_mh.group(1)) if _mh else NH
    for c in range(_nc_lim):
        KT = 4 * (c + 1)
        t_xc = dma('g', xc, bass.AP(xT, c * CW, [[S, 128], [128 * S, NDT], [1, CW]]),
                   waits=((pe, pe.n),))
        for kv in range(NKV):
            tws = dma('g', wstg[kv % 2],
                      bass.AP(wkT, kv * 128, [[NKV * HD, 128],
                                              [128 * NKV * HD, 32], [1, 128]]),
                      waits=((pe, g_wstg[kv % 2]),))
            for d in range(NDT):
                w = [(dg, tws), (dg, t_xc), g_PJ] if d == 0 else ()
                tk = mm(PJ[:, :], wstg[kv % 2][:, d * 128:(d + 1) * 128],
                        xc[:, d * CW:(d + 1) * CW], start=(d == 0),
                        stop=(d == NDT - 1), waits=w)
            g_wstg[kv % 2] = tk
            t_kc = sop(lambda e: e.copy(kraw, PJ[:, :]),
                       waits=((pe, tk), (dv, g_invex)))
            t_k2 = sop(lambda e: e.activation(out=k2b, in_=kraw, func=AF.Square))
            tsum = mm(PR[0:1, :], onescol[:, :], k2b, start=True, stop=True,
                      waits=[(ac, t_k2), g_PR])
            t_ln, t_ex = rms_rows(tsum)
            g_PR = (ac, t_ln)
            tex = mm(PX[:, :], onesrow[:, :], rowb[0:1, 0:CW], start=True, stop=True,
                     waits=((ac, t_ex), (ac, g_PX)))
            g_rows_pe = tex
            t_ic = sop(lambda e: e.copy(invex, PX[:, :]),
                       waits=((pe, tex), (dv, g_invex)))
            g_PX = t_ic
            t_kh = vop(lambda e, kvv=kv, cc=c: e.scalar_tensor_tensor(
                out=Khat[:, kvv * 2048 + cc * CW: kvv * 2048 + (cc + 1) * CW],
                in0=PJ[:, :], scalar=knorm[:, :], in1=invex, op0=ALU.mult, op1=ALU.mult),
                waits=((ac, t_ic),))
            g_invex = t_kh
            g_PJ = (dv, t_kh)
            t_khs = settle()
            t_k2n = sop(lambda e, kvv=kv, cc=c: e.activation(
                out=k2b, in_=Khat[:, kvv * 2048 + cc * CW: kvv * 2048 + (cc + 1) * CW],
                func=AF.Square), waits=((dv, t_khs),))
            tsum = mm(PR[0:1, :], onescol[:, :], k2b, start=True, stop=True,
                      waits=[(ac, t_k2n), g_PR])
            t_mx = vop(lambda e: e.tensor_reduce(rowf[0:1, CW:CW + 1], PR[0:1, :],
                                                 AXX, ALU.max), waits=((pe, tsum),))
            g_PR = (dv, t_mx)
            if c == 0:
                vop(lambda e, kvv=kv: e.tensor_copy(maxk[0:1, kvv:kvv + 1],
                                                    rowf[0:1, CW:CW + 1]))
            else:
                vop(lambda e, kvv=kv: e.tensor_tensor(
                    out=maxk[0:1, kvv:kvv + 1], in0=maxk[0:1, kvv:kvv + 1],
                    in1=rowf[0:1, CW:CW + 1], op=ALU.max))

        for d in range(NDT):
            tvw = dma('g', wvstg[d % 2],
                      bass.AP(wvT, d * 128 * NKV * HD, [[NKV * HD, 128], [1, NKV * HD]]),
                      waits=((pe, pe.n if d < 2 else v_stage[d % 2]),))
            for sub in range(4):
                w = []
                if d == 0 and sub == 0:
                    w = [g_PJ, (ac, g_PX), (dg, tvw)]
                elif sub == 0:
                    w = [(dg, tvw)]
                tkk = mm(PV[sub][:, :], xc[:, d * CW + sub * 128: d * CW + sub * 128 + 128],
                         wvstg[d % 2], start=(d == 0), stop=(d == NDT - 1),
                         waits=w, tok=(sub == 3))
            v_stage[d % 2] = tkk
        for sub in range(4):
            tvc = sop(lambda e, ss=sub, cc=c: e.copy(
                Vb[:, (4 * cc + ss) * CW: (4 * cc + ss + 1) * CW], PV[ss][:, :]),
                waits=((pe, pe.n),))
            v_tok[4 * c + sub] = tvc
        g_PJ = (ac, tvc)
        g_PX = tvc

        for h in range(_nh_lim):
            kv = h // 4
            tws = dma('g', wstg[h % 2],
                      bass.AP(wqT, h * 128, [[NH * HD, 128], [128 * NH * HD, 32], [1, 128]]),
                      waits=((pe, g_wstg[h % 2]),))
            for d in range(NDT):
                w = [(dg, tws), g_PJ] if d == 0 else ()
                tq = mm(PJ[:, :], wstg[h % 2][:, d * 128:(d + 1) * 128],
                        xc[:, d * CW:(d + 1) * CW], start=(d == 0),
                        stop=(d == NDT - 1), waits=w)
            g_wstg[h % 2] = tq
            t_qc = sop(lambda e: e.copy(kraw, PJ[:, :]), waits=((pe, tq), (dv, g_invex)))
            t_q2 = sop(lambda e: e.activation(out=k2b, in_=kraw, func=AF.Square))
            tsum = mm(PR[0:1, :], onescol[:, :], k2b, start=True, stop=True,
                      waits=[(ac, t_q2), g_PR])
            t_ln, t_ex = rms_rows(tsum)
            g_PR = (ac, t_ln)
            tex = mm(PX[:, :], onesrow[:, :], rowb[0:1, 0:CW], start=True, stop=True,
                     waits=((ac, t_ex), (ac, g_PX)))
            g_rows_pe = tex
            t_ic = sop(lambda e: e.copy(invex, PX[:, :]), waits=((pe, tex), (dv, g_invex)))
            g_PX = t_ic
            t_qh = vop(lambda e: e.scalar_tensor_tensor(
                out=qhat, in0=PJ[:, :], scalar=qnorm[:, :], in1=invex,
                op0=ALU.mult, op1=ALU.mult), waits=((ac, t_ic), (pe, g_qhat)))
            g_invex = t_qh
            g_PJ = (dv, t_qh)
            t_qhs = settle()
            t_qn2 = sop(lambda e: e.activation(out=k2b, in_=qhat, func=AF.Square),
                        waits=((dv, t_qhs),))
            tsum = mm(PR[0:1, :], onescol[:, :], k2b, start=True, stop=True,
                      waits=[(ac, t_qn2), g_PR])
            t_sr = dma('y', srow_t[0:1, :],
                       bass.AP(srow_in, (h * NCHUNK + c) * CW, [[CW, 1], [1, CW]]),
                       waits=((dv, g_srow),))
            t_s1 = vop(lambda e, kvv=kv: e.tensor_tensor(
                out=rowf[0:1, CW:2 * CW], in0=PR[0:1, :],
                in1=maxk[0:1, kvv:kvv + 1].broadcast_to([1, CW]), op=ALU.add),
                waits=((pe, tsum),))
            vop(lambda e: e.tensor_scalar(
                out=rowf[0:1, CW:2 * CW], in0=rowf[0:1, CW:2 * CW], scalar1=-0.5,
                scalar2=None, op0=ALU.mult))
            t_s2 = vop(lambda e: e.tensor_tensor(
                out=rowf[0:1, CW:2 * CW], in0=rowf[0:1, CW:2 * CW],
                in1=srow_t[0:1, :], op=ALU.subtract), waits=((dy, t_sr),))
            t_ss = settle()
            g_srow = t_s2
            g_PR = (dv, t_s1)
            t_sb = sop(lambda e: e.copy(rowb[0:1, CW:2 * CW], rowf[0:1, CW:2 * CW]),
                       waits=((dv, t_ss), (pe, g_rows_pe)))
            g_rows_pe = pe.n

            exp_tok = [0] * KT
            for kt in range(KT):
                qs = 128 * (kt - 4 * c) if kt >= 4 * c else 0
                crossing = kt >= 4 * c
                w = [(dv, t_qhs)]
                if kt >= 2:
                    w.append((ac, exp_tok[kt - 2]))
                if kt == 0:
                    w.append((ac, t_sb))
                mm(PS[kt % 2][:, qs:CW],
                   Khat[:, kv * 2048 + kt * 128: kv * 2048 + (kt + 1) * 128],
                   qhat[:, qs:CW], start=True, stop=False, waits=w)
                ttok = mm(PS[kt % 2][:, qs:CW], onesrow[:, :],
                          rowb[0:1, CW + qs:2 * CW], start=False, stop=not crossing)
                if crossing:
                    ttok = mm(PS[kt % 2][:, qs:qs + 128], ident[:, :], trimask[:, :],
                              start=False, stop=True)
                wexp = [(pe, ttok)]
                if kt == 0:
                    wexp.append((pe, g_probs))
                exp_tok[kt] = sop(lambda e, k=kt, q=qs, hh=h: e.activation(
                    out=probs[:, k * CW + q: (k + 1) * CW], in_=PS[k % 2][:, q:CW],
                    func=AF.Exp, bias=bcol[:, hh * 16 + k: hh * 16 + k + 1], scale=RSQ),
                    waits=wexp)
            for kt in range(KT):
                qs = 128 * (kt - 4 * c) if kt >= 4 * c else 0
                w = [(ac, exp_tok[kt])]
                if kt == 0:
                    w += [g_PR, (ac, v_tok[4 * (c + 1) - 1]), (dv, g_PA)]
                mm(PR[0:1, qs:CW], onescol[:, :], probs[:, kt * CW + qs:(kt + 1) * CW],
                   start=(kt == 0), stop=(kt == KT - 1), waits=w)
                tav = mm(PA[:, qs:CW], Vb[:, kt * CW + kv * 128: kt * CW + kv * 128 + 128],
                         probs[:, kt * CW + qs:(kt + 1) * CW],
                         start=(kt == 0), stop=(kt == KT - 1))
            g_probs = tav
            t_rc = vop(lambda e: e.reciprocal(rowf[0:1, 2 * CW:3 * CW], PR[0:1, :]),
                       waits=((pe, pe.n),))
            g_PR = (dv, t_rc)
            t_rcs = settle()
            t_rb = sop(lambda e: e.copy(rowb[0:1, 2 * CW:3 * CW], rowf[0:1, 2 * CW:3 * CW]),
                       waits=((dv, t_rcs), (pe, g_rows_pe)))
            tex = mm(PX[:, :], onesrow[:, :], rowb[0:1, 2 * CW:3 * CW],
                     start=True, stop=True, waits=((ac, t_rb), (ac, g_PX)))
            g_rows_pe = tex
            t_rcp = sop(lambda e: e.copy(recb, PX[:, :]), waits=((pe, tex),))
            g_PX = t_rcp
            t_av = vop(lambda e: e.tensor_tensor(out=avb, in0=PA[:, :], in1=recb,
                                                 op=ALU.mult),
                       waits=((ac, t_rcp), (dy, g_avb), (pe, tav)))
            g_PA = t_av
            g_qhat = tav
            t_avs = settle()

            g_avb = dma('y', bass.AP(avmine, (hh_cc := h * NCHUNK + c) * AVPIECE,
                                     [[CW, 128], [1, CW]]),
                        avb, waits=((dv, t_avs),))

    def f_avx(e):
        off = goff(e, AVREGION, 0)
        return e.dma_start(out=bass.AP(avx, off, [[1, AVREGION]]),
                           in_=bass.AP(avmine, 0, [[1, AVREGION]]))
    P.e('g', f_avx, ((dy, dy.n),), (dg, 16), chain=(dg, dg.n))
    dg.n += 16
    t_avx = dg.n

    P.e('y', None, ((dy, dy.n),))
    P.e('g', None, ((dg, dg.n),))
    P.emit(nc)
    P = Plan()

    if phases == 'AB':
        P.e('g', lambda e: e.dma_start(out=out_xt.ap(), in_=bass.AP(avmine, 0, [[T, DM], [1, T]])), ((cag1, 1), (cag2, 1), (dg, dg.n)), (dg, 16))
        dg.n += 16
        P.e('y', None, ((dy, dy.n),))
        P.emit(nc)
        return nc
    vbuf = AR(0, 64 * KB, BF16)
    wostg = [AR(64 * KB + i * KB, 1 * KB, BF16) for i in range(2)]
    avjs = [AR(66 * KB + i * 2 * KB, 2 * KB, BF16) for i in range(2)]
    v2b = [AR(70 * KB + i * KB, 1 * KB, BF16) for i in range(2)]
    expx = AR(72 * KB, 2 * KB, F32)
    expy = AR(74 * KB, 2 * KB, F32)
    xob = [AR(76 * KB + i * KB, 1 * KB, BF16) for i in range(2)]
    xot = [AR(78 * KB + i * KB, 1 * KB, BF16) for i in range(2)]

    tb = vop(lambda e: e.memset(bar_sb[0:1, :], 1.0))
    tbw = dma('y', bar_i.ap(), bar_sb[0:1, :], waits=((dv, tb),))
    P.e('g', lambda e: e.collective_compute(
        "AllReduce", ALU.add, replica_groups=[[0, 1], [2, 3], [4, 5], [6, 7]],
        ins=[bar_i.ap().opt()], outs=[bar_o.ap().opt()]),
        ((dy, tbw), (dg, t_avx)), (cbar, 1))
    cbar.n += 1
    for pj in range(2):
        for cp in range(2):
            def f_gather(e, pj_=pj, cp_=cp):
                off = goff(e, 2 * AVPIECE, pj_ * AVREGION + cp_ * AVPIECE)
                src = bass.AP(avx, off, [[4 * AVPIECE, 16], [CW, 128], [1, CW]])
                dst = bass.AP(avloc, pj_ * 16 * 2 * AVPIECE + cp_ * AVPIECE,
                              [[2 * AVPIECE, 16], [CW, 128], [1, CW]])
                return e.dma_start(out=dst, in_=src)
            P.e('g', f_gather, ((cbar, 1),), (dg, 16), chain=(dg, dg.n))
            dg.n += 16
    t_avloc = dg.n

    g_stg = [0, 0]
    g_avj = [0, 0]
    ev_dv = 0
    tkn = 0
    for cs in range(8):
        ev_prev = ev_dv
        for j in range(NDT):
            tws = dma('g', wostg[j % 2],
                      bass.AP(woT, j * 128 * DM + cs * 512, [[DM, 128], [1, 512]]),
                      waits=((pe, g_stg[j % 2]),))

            def f_avr(e, jj=j, buf=avjs[j % 2]):
                return e.dma_start(
                    out=buf, in_=bass.AP(avloc, jj * 2 * AVPIECE,
                                         [[CW, 128], [AVPIECE, 2], [1, CW]]))
            P.e('g', f_avr, ((dg, t_avloc), (pe, g_avj[j % 2])), (dg, 16),
                chain=(dg, dg.n))
            dg.n += 16
            tavr = dg.n
            for ct in range(4):
                for th in range(2):
                    w = []
                    if ct == 0 and th == 0:
                        w = [(dg, tws), (dg, tavr)]
                        if j == 0:
                            w.append((dv, ev_prev))
                    tkn = mm(pbk[ct * 2 + th][:, :], wostg[j % 2][:, ct * 128:(ct + 1) * 128],
                             avjs[j % 2][:, th * CW:(th + 1) * CW],
                             start=(j == 0), stop=(j == NDT - 1),
                             waits=w, tok=(ct == 3 and th == 1))
            g_stg[j % 2] = tkn
            g_avj[j % 2] = tkn
        for ct in range(4):
            ctg = cs * 4 + ct
            for th in range(2):
                ev_dv = vop(lambda e, a=pbk[ct * 2 + th], cg=ctg, t=th: e.tensor_tensor(
                    out=vbuf[:, cg * T + t * CW: cg * T + (t + 1) * CW],
                    in0=a[:, :], in1=residual[:, cg * T + t * CW: cg * T + (t + 1) * CW],
                    op=ALU.add), waits=((pe, tkn),))

    ev_s = settle()
    sq_tok = [0, 0]
    for ctg in range(NDT):
        for th in range(2):
            w = []
            if ctg == 0:
                w = [(dv, ev_s)]
            else:
                w = [(pe, sq_tok[th])]
            t_v2 = sop(lambda e, cg=ctg, t=th, b=v2b[th]: e.activation(
                out=b, in_=vbuf[:, cg * T + t * CW: cg * T + (t + 1) * CW],
                func=AF.Square), waits=w)
            mm(pbk[0 + th][0:1, :], onescol[:, :],
               vbuf[:, ctg * T + th * CW: ctg * T + (th + 1) * CW],
               start=(ctg == 0), stop=(ctg == NDT - 1),
               waits=((dv, ev_s),) if ctg == 0 and th == 0 else ())
            sq_tok[th] = mm(pbk[2 + th][0:1, :], onescol[:, :], v2b[th],
                            start=(ctg == 0), stop=(ctg == NDT - 1),
                            waits=((ac, t_v2),), tok=True)
    t_stats = pe.n

    g_rbc = 0
    g_rowdma = 0
    t_xo = [0, 0]
    for th in range(2):
        tm = vop(lambda e, t=th: e.tensor_scalar(
            out=rowf[0:1, 0:CW], in0=pbk[t][0:1, :], scalar1=1.0 / DM, scalar2=None,
            op0=ALU.mult), waits=((pe, t_stats), (ac, ac.n), (dy, g_rowdma)))
        vop(lambda e, t=th: e.tensor_scalar(
            out=rowf[0:1, CW:2 * CW], in0=pbk[2 + t][0:1, :], scalar1=1.0 / DM,
            scalar2=None, op0=ALU.mult))
        vop(lambda e: e.tensor_tensor(out=rowf[0:1, 2 * CW:3 * CW], in0=rowf[0:1, 0:CW],
                                      in1=rowf[0:1, 0:CW], op=ALU.mult))
        t_var = vop(lambda e: e.tensor_tensor(
            out=rowf[0:1, CW:2 * CW], in0=rowf[0:1, CW:2 * CW],
            in1=rowf[0:1, 2 * CW:3 * CW], op=ALU.subtract))
        t_vars = settle()
        sop(lambda e: e.activation(out=rowf[0:1, 2 * CW:3 * CW], in_=rowf[0:1, CW:2 * CW],
                                   func=AF.Ln, bias=eps5[0:1, 0:1], scale=1.0), waits=((dv, t_vars),))
        t_rs = sop(lambda e: e.activation(out=rowf[0:1, 3 * CW:4 * CW],
                                          in_=rowf[0:1, 2 * CW:3 * CW],
                                          func=AF.Exp, scale=-0.5))
        t_mr = vop(lambda e: e.tensor_tensor(out=rowf[0:1, CW:2 * CW], in0=rowf[0:1, 0:CW],
                                             in1=rowf[0:1, 3 * CW:4 * CW], op=ALU.mult),
                   waits=((ac, t_rs),))
        t_mrs = settle()
        t_w1 = dma('y', bass.AP(rscr, 0, [[CW, 1], [1, CW]]),
                   rowf[0:1, 3 * CW:4 * CW], waits=((ac, t_rs), (dg, g_rbc), (dv, t_mrs)))
        t_w2 = dma('y', bass.AP(rscr, CW, [[CW, 1], [1, CW]]), rowf[0:1, CW:2 * CW])
        tb1 = dma('g', expx, bass.AP(rscr, 0, [[0, 128], [1, CW]]),
                  waits=((dy, t_w2), (dv, dv.n)))
        tb2 = dma('g', expy, bass.AP(rscr, CW, [[0, 128], [1, CW]]))
        g_rbc = tb2
        g_rowdma = t_w2
        for ct in range(NDT):
            w = [(dg, tb2)]
            if ct >= 2:
                w.append((dy, t_xo[ct % 2]))
            t_m1 = vop(lambda e, cg=ct, t=th, b=xob[ct % 2]: e.tensor_tensor(
                out=b, in0=vbuf[:, cg * T + t * CW: cg * T + (t + 1) * CW],
                in1=expx, op=ALU.mult), waits=w)
            t_m2 = vop(lambda e, b1=xob[ct % 2], b2=xot[ct % 2]: e.tensor_tensor(
                out=b2, in0=b1, in1=expy, op=ALU.subtract))
            t_m2s = settle()
            t_xo[ct % 2] = dma('y', bass.AP(x2d, ct * 128 * T + th * CW,
                                            [[T, 128], [1, CW]]),
                               xot[ct % 2], waits=((dv, t_m2s),))
    P.e('y', None, ((dy, dy.n),))
    P.e('g', None, ((dg, dg.n),))
    P.emit(nc)
    P = Plan()

    if phases == 'ABC':
        P.e('g', lambda e: e.dma_start(out=out_xt.ap(), in_=bass.AP(x2d, 0, [[1, DM * T]])), ((cag1, 1), (cag2, 1), (dy, dy.n)), (dg, 16))
        dg.n += 16
        P.e('y', None, ((dy, dy.n),))
        P.emit(nc)
        return nc
    x2tc = AR(0, 32 * KB, BF16)
    hbuf = AR(32 * KB, 86 * KB, BF16)
    pstg = [RS(0 + i * 16 * KB, 16 * KB, BF16) for i in range(2)]
    vbufD = RS(0, 32 * KB, BF16)
    wdstg = [RS(32 * KB + i * KB, 1 * KB, BF16) for i in range(2)]
    sgb = [RS(34 * KB + i * KB, 1 * KB, BF16) for i in range(2)]
    vb2d = [RS(36 * KB + i * KB, 1 * KB, BF16) for i in range(2)]
    expxD = RS(38 * KB, 2 * KB, F32)
    expyD = RS(40 * KB, 2 * KB, F32)
    oflD = [RS(42 * KB + i * 2 * KB, 2 * KB, F32) for i in range(2)]
    oflD2 = [RS(46 * KB + i * 2 * KB, 2 * KB, F32) for i in range(2)]

    g_pe_tc = 0
    g_rbcD = 0
    g_rowdmaD = 0
    for tc in range(2):
        t_x2 = dma('y', x2tc, bass.AP(x2d, tc * CW, [[T, 128], [128 * T, NDT], [1, CW]]),
                   waits=((pe, g_pe_tc),))

        PG = [pbk[0], pbk[1]]
        PU = [pbk[2], pbk[3]]
        sil_tok = [0] * NPAIR
        h_tok = [0] * NPAIR
        pe_pair = [g_pe_tc, g_pe_tc]
        for p in range(NPAIR):
            wmn = [(pe, pe_pair[p % 2])]
            if tc == 1 and p < 2:
                wmn.append((dy, g_ofl_dy))
                wmn.append((dv, g_vnorm))
            tmn = dma('y' if p % 2 == 0 else 'g', pstg[p % 2],
                      bass.AP(ggu, p * GROW, [[256, 128], [128 * 256, 32], [1, 256]]),
                      waits=wmn)
            tmn_ch = dy if p % 2 == 0 else dg
            for d in range(NDT):
                w = []
                if d == 0:
                    w = [(tmn_ch, tmn), (dy, t_x2)]
                    if p == 0:
                        w.append((cag1, 1))
                        w.append((cag2, 1))
                    if p >= 2:
                        w.append((ac, sil_tok[p - 2]))
                mm(PG[p % 2][:, :], pstg[p % 2][:, d * 256: d * 256 + 128],
                   x2tc[:, d * CW:(d + 1) * CW], start=(d == 0), stop=(d == NDT - 1),
                   waits=w)
            for d in range(NDT):
                w = []
                if d == 0 and p >= 2:
                    w = [(dv, h_tok[p - 2])]
                tup = mm(PU[p % 2][:, :], pstg[p % 2][:, d * 256 + 128: d * 256 + 256],
                         x2tc[:, d * CW:(d + 1) * CW], start=(d == 0),
                         stop=(d == NDT - 1), waits=w)
            pe_pair[p % 2] = tup
            sil_tok[p] = sop(lambda e, pp=p: e.activation(
                out=sgb[pp % 2], in_=PG[pp % 2][:, :], func=AF.Silu),
                waits=((pe, tup),))
            h_tok[p] = vop(lambda e, pp=p: e.tensor_tensor(
                out=hbuf[:, pp * CW:(pp + 1) * CW], in0=sgb[pp % 2],
                in1=PU[pp % 2][:, :], op=ALU.mult), waits=((ac, sil_tok[p]),))
        t_hs = settle()

        PD = [pbk[0], pbk[1], pbk[2], pbk[3]]
        g_wd = [0, 0]
        ev2 = 0
        for cs in range(8):
            ev_p = ev2
            for p in range(NPAIR):
                w = [(pe, g_wd[p % 2])]
                if cs == 0 and p < 2:
                    w.append((dv, t_hs))
                twd = dma('y' if p % 2 == 0 else 'g', wdstg[p % 2],
                          bass.AP(gd, p * DROW + cs * 512, [[4096, 128], [1, 512]]),
                          waits=w)
                twd_ch = dy if p % 2 == 0 else dg
                for j in range(4):
                    wj = []
                    if j == 0:
                        wj = [(twd_ch, twd)]
                        if p == 0:
                            wj += [(dv, t_hs), (dv, ev_p)]
                    tdn = mm(PD[j][:, :], wdstg[p % 2][:, j * 128:(j + 1) * 128],
                             hbuf[:, p * CW:(p + 1) * CW], start=(p == 0),
                             stop=(p == NPAIR - 1), waits=wj, tok=(j == 3))
                g_wd[p % 2] = tdn
            for j in range(4):
                ctg = cs * 4 + j
                ev2 = vop(lambda e, a=PD[j], cg=ctg: e.tensor_tensor(
                    out=vbufD[:, cg * CW:(cg + 1) * CW], in0=a[:, :],
                    in1=x2tc[:, cg * CW:(cg + 1) * CW], op=ALU.add),
                    waits=((pe, tdn),))

        ev2s = settle()
        sq2 = 0
        for ctg in range(NDT):
            t_v2 = sop(lambda e, cg=ctg, b=vb2d[ctg % 2]: e.activation(
                out=b, in_=vbufD[:, cg * CW:(cg + 1) * CW], func=AF.Square),
                waits=((dv, ev2s),) if ctg == 0 else ((pe, sq2),))
            mm(pbk[4][0:1, :], onescol[:, :], vbufD[:, ctg * CW:(ctg + 1) * CW],
               start=(ctg == 0), stop=(ctg == NDT - 1),
               waits=((dv, ev2s),) if ctg == 0 else ())
            sq2 = mm(pbk[5][0:1, :], onescol[:, :], vb2d[ctg % 2],
                     start=(ctg == 0), stop=(ctg == NDT - 1),
                     waits=((ac, t_v2),), tok=True)
        t_st2 = pe.n

        tm = vop(lambda e: e.tensor_scalar(
            out=rowf[0:1, 0:CW], in0=pbk[4][0:1, :], scalar1=1.0 / DM, scalar2=None,
            op0=ALU.mult), waits=((pe, t_st2), (ac, ac.n), (dy, g_rowdmaD)))
        vop(lambda e: e.tensor_scalar(
            out=rowf[0:1, CW:2 * CW], in0=pbk[5][0:1, :], scalar1=1.0 / DM, scalar2=None,
            op0=ALU.mult))
        vop(lambda e: e.tensor_tensor(out=rowf[0:1, 2 * CW:3 * CW], in0=rowf[0:1, 0:CW],
                                      in1=rowf[0:1, 0:CW], op=ALU.mult))
        t_var = vop(lambda e: e.tensor_tensor(
            out=rowf[0:1, CW:2 * CW], in0=rowf[0:1, CW:2 * CW],
            in1=rowf[0:1, 2 * CW:3 * CW], op=ALU.subtract))
        t_vars = settle()
        sop(lambda e: e.activation(out=rowf[0:1, 2 * CW:3 * CW], in_=rowf[0:1, CW:2 * CW],
                                   func=AF.Ln, bias=eps5[0:1, 0:1], scale=1.0), waits=((dv, t_vars),))
        t_rs = sop(lambda e: e.activation(out=rowf[0:1, 3 * CW:4 * CW],
                                          in_=rowf[0:1, 2 * CW:3 * CW],
                                          func=AF.Exp, scale=-0.5))
        t_mr = vop(lambda e: e.tensor_tensor(out=rowf[0:1, CW:2 * CW], in0=rowf[0:1, 0:CW],
                                             in1=rowf[0:1, 3 * CW:4 * CW], op=ALU.mult),
                   waits=((ac, t_rs),))
        t_mrs = settle()
        t_w1 = dma('y', bass.AP(rscr, 0, [[CW, 1], [1, CW]]),
                   rowf[0:1, 3 * CW:4 * CW], waits=((ac, t_rs), (dg, g_rbcD), (dv, t_mrs)))
        t_w2 = dma('y', bass.AP(rscr, CW, [[CW, 1], [1, CW]]), rowf[0:1, CW:2 * CW])
        tb1 = dma('g', expxD, bass.AP(rscr, 0, [[0, 128], [1, CW]]),
                  waits=((dy, t_w2), (dv, dv.n)))
        tb2 = dma('g', expyD, bass.AP(rscr, CW, [[0, 128], [1, CW]]))
        g_rbcD = tb2
        g_rowdmaD = t_w2
        t_ot = [0, 0]
        for ct in range(NDT):
            w = [(dg, tb2)]
            if ct >= 2:
                w.append((dy, t_ot[ct % 2]))
            t_m1 = vop(lambda e, cg=ct, b=oflD[ct % 2]: e.tensor_tensor(
                out=b, in0=vbufD[:, cg * CW:(cg + 1) * CW], in1=expxD, op=ALU.mult),
                waits=w)
            t_m2 = vop(lambda e, b=oflD[ct % 2], b2=oflD2[ct % 2]: e.tensor_tensor(
                out=b2, in0=b, in1=expyD, op=ALU.subtract))
            t_m2s = settle()
            t_ot[ct % 2] = dma('y', bass.AP(out_xt, ct * 128 * T + tc * CW,
                                            [[T, 128], [1, CW]]),
                               oflD2[ct % 2], waits=((dv, t_m2s),))
        g_pe_tc = pe.n
        g_ofl_dy = dy.n
        g_vnorm = dv.n

    P.e('y', None, ((dy, dy.n),))
    P.e('g', None, ((dg, dg.n),))
    P.emit(nc)
    return nc


def prepare_in_maps(L0_attn_x, L0_attn_Wq, L0_attn_Wk, L0_attn_Wv, L0_attn_Wo,
                    L0_attn_q_norm_w, L0_attn_k_norm_w,
                    L0_ffn_qweight_gu, L0_ffn_qzeros_gu, L0_ffn_scales_gu,
                    L0_ffn_g_idx_gu, L0_ffn_qweight_down, L0_ffn_qzeros_down,
                    L0_ffn_scales_down, L0_ffn_g_idx_down):
    x = np.ascontiguousarray(np.asarray(L0_attn_x, dtype=np.float32))
    Wq = np.asarray(L0_attn_Wq, dtype=np.float32)
    Wk = np.asarray(L0_attn_Wk, dtype=np.float32)
    Wv = np.asarray(L0_attn_Wv, dtype=np.float32)
    Wo = np.asarray(L0_attn_Wo, dtype=np.float32)
    qn = np.asarray(L0_attn_q_norm_w, dtype=np.float32)
    kn = np.asarray(L0_attn_k_norm_w, dtype=np.float32)
    qwg = np.ascontiguousarray(np.asarray(L0_ffn_qweight_gu))
    qzg = np.ascontiguousarray(np.asarray(L0_ffn_qzeros_gu))
    scg = np.ascontiguousarray(np.asarray(L0_ffn_scales_gu, dtype=np.float32))
    qwd = np.ascontiguousarray(np.asarray(L0_ffn_qweight_down))
    qzd = np.ascontiguousarray(np.asarray(L0_ffn_qzeros_down))
    scd = np.ascontiguousarray(np.asarray(L0_ffn_scales_down, dtype=np.float32))
    assert np.array_equal(np.asarray(L0_ffn_g_idx_gu), np.arange(4096) // 128)
    assert np.array_equal(np.asarray(L0_ffn_g_idx_down), np.arange(11008) // 128)

    wqT = np.ascontiguousarray(Wq.T)
    wkT = np.ascontiguousarray(Wk.T)
    wvT = np.ascontiguousarray(Wv.T)
    woT = np.ascontiguousarray(Wo.T)

    slopes = _slopes()
    iotac = np.arange(128, dtype=np.float32)
    trimask = np.where(iotac[:, None] > iotac[None, :], np.float32(NEG),
                       np.float32(0.0)).astype(ml_dtypes.bfloat16)
    ident128 = np.eye(128, dtype=ml_dtypes.bfloat16)
    onesrow = np.ones((1, 128), dtype=ml_dtypes.bfloat16)
    onescol = np.ones((128, 1), dtype=ml_dtypes.bfloat16)
    _sh = (np.arange(128) % 8) * 4
    maskcol = (np.uint32(15) << _sh.astype(np.uint32)).reshape(128, 1)
    pw2col = (2.0 ** (-_sh.astype(np.float64))).astype(np.float32).reshape(128, 1)

    units = [[p for p in range(NPAIR) if p // UPC == r] for r in range(8)]

    in_maps = []
    for core in range(8):
        b = core // 2
        par = core % 2
        h0 = 16 * par
        kv0 = 4 * par
        xb = np.ascontiguousarray(x[b].T)

        srow = np.zeros((NH * NCHUNK, CW), dtype=np.float32)
        bcol = np.zeros((128, NH * 16), dtype=np.float32)
        for hh in range(NH):
            sl = float(slopes[h0 + hh])
            for c in range(NCHUNK):
                srow[hh * NCHUNK + c, :] = sl * (c * CW + np.arange(CW, dtype=np.float32)) * np.float32(math.sqrt(128.0))
            for kt in range(16):
                bcol[:, hh * 16 + kt] = sl * (128 * kt + np.arange(128, dtype=np.float32))

        ups = units[core]
        qw_gu_t = np.zeros((512, 256 * UPC), dtype=np.uint32)
        sc_gu_s = np.zeros((32, 256 * UPC), dtype=np.float32)
        qz_gu_s = np.zeros((32, 32 * UPC), dtype=np.uint32)
        qw_d_t = np.zeros((16 * UPC, 4096), dtype=np.uint32)
        sc_d_s = np.zeros((UPC, 4096), dtype=np.float32)
        qz_d_s = np.zeros((UPC, 512), dtype=np.uint32)
        for i, pp in enumerate(ups):
            gc = slice(128 * pp, 128 * pp + 128)
            uc = slice(128 * (pp + NPAIR), 128 * (pp + NPAIR) + 128)
            qw_gu_t[:, 256 * i:256 * i + 128] = qwg[:, gc]
            qw_gu_t[:, 256 * i + 128:256 * i + 256] = qwg[:, uc]
            sc_gu_s[:, 256 * i:256 * i + 128] = scg[:, gc]
            sc_gu_s[:, 256 * i + 128:256 * i + 256] = scg[:, uc]
            gw = slice(16 * pp, 16 * pp + 16)
            uw = slice(16 * (pp + NPAIR), 16 * (pp + NPAIR) + 16)
            qz_gu_s[:, 32 * i:32 * i + 16] = qzg[:, gw]
            qz_gu_s[:, 32 * i + 16:32 * i + 32] = qzg[:, uw]
            qw_d_t[16 * i:16 * i + 16, :] = qwd[16 * pp:16 * pp + 16, :]
            sc_d_s[i, :] = scd[pp, :]
            qz_d_s[i, :] = qzd[pp, :]
        qw_gu_s = np.ascontiguousarray(np.repeat(qw_gu_t, 8, axis=0))
        qw_d_s = np.ascontiguousarray(np.repeat(qw_d_t, 8, axis=0))

        in_maps.append(dict(
            xT_b=xb,
            xown=np.ascontiguousarray(xb[:, par * T:(par + 1) * T]),
            wqT_s=np.ascontiguousarray(wqT[:, h0 * 128:(h0 + NH) * 128]),
            wkT_s=np.ascontiguousarray(wkT[:, kv0 * 128:(kv0 + NKV) * 128]),
            wvT_s=np.ascontiguousarray(wvT[:, kv0 * 128:(kv0 + NKV) * 128]),
            woT=woT,
            qnorm=qn.reshape(128, 1).astype(np.float32),
            knorm=kn.reshape(128, 1).astype(np.float32),
            srow_in=srow, bcol_in=bcol, trimask=trimask, ident128=ident128,
            onesrow=onesrow, onescol=onescol, maskcol=maskcol, pw2col=pw2col,
            qw_gu_s=qw_gu_s, sc_gu_s=sc_gu_s, qz_gu_s=qz_gu_s,
            qw_d_s=qw_d_s, sc_d_s=sc_d_s, qz_d_s=qz_d_s,
        ))

    return in_maps


def assemble_out(results):
    out = np.empty((B, S, DM), dtype=np.float32)
    for core in range(8):
        b = core // 2
        par = core % 2
        o = np.asarray(results[core]["out_xt"], dtype=np.float32)
        out[b, par * T:(par + 1) * T, :] = o.T
    return out


def kernel(**inputs):
    in_maps = prepare_in_maps(**inputs)
    nc = build_program()
    res = run_bass_kernel_spmd(nc, in_maps, core_ids=list(range(8)))
    return assemble_out(res.results)
